# version 21
# baseline (speedup 1.0000x reference)
import os
import numpy as np
import ml_dtypes
from contextlib import ExitStack

import concourse.bass as bass
import concourse.mybir as mybir
from concourse.bass_utils import run_bass_kernel_spmd
from bass_rust import IndirectOffsetOnAxis

F32 = mybir.dt.float32
BF16 = mybir.dt.bfloat16
U32 = mybir.dt.uint32
AF = mybir.ActivationFunctionType
ALU = mybir.AluOpType
AX = mybir.AxisListType

NCORE = 8
T = 2048
D = 1024
FC = 8
NT = 16
E = 8
CAP = 2560
DFF = 4096
DSH = 8192
LN_EPS = 1e-5
WKV_EPS = 1e-8
RMS_EPS = float(np.finfo(np.float32).eps)
BIG = float(2 << 29)

_STAGE = "full"

ENGINES = ("sync", "vector", "scalar", "tensor", "gpsimd")


class Sem:
    __slots__ = ("h", "n", "name")

    def __init__(self, handle, name):
        self.h = handle
        self.n = 0
        self.name = name

    def tok(self):
        return (self, self.n)


class Prog:

    def __init__(self, nc):
        self.nc = nc
        self.steps = {e: [] for e in ENGINES}
        self.auto = {}
        self.streams = {}
        self._mkstream = None

    def ssem(self, name):
        if name not in self.streams:
            self.streams[name] = self._mkstream(name)
        return self.streams[name]

    def op(self, eng, fn, waits=(), sem=None, amt=1, chain=True):
        waits = tuple(w for w in waits if w is not None)
        for s, v in waits:
            assert v <= s.n, f"wait on future value of {s.name}: {v} > {s.n}"
        a = self.auto.get(eng) if chain else None
        if a is not None:
            if a.n > 0:
                waits = waits + ((a, a.n),)
            a.n += 1
            self.steps[eng].append((waits, fn, ((a, 1),)))
            return a.tok() if sem is not None else None
        incs = ()
        if sem is not None:
            sem.n += amt
            incs = ((sem, amt),)
        self.steps[eng].append((waits, fn, incs))
        return sem.tok() if sem is not None else None

    def dma(self, eng, out, in_, waits=(), sem=None, stream=None, **kw):
        assert stream is not None, "p.dma requires stream="
        sem = self.ssem(stream)
        def fn(e, out=out, in_=in_, kw=kw):
            return e.dma_start(out=out, in_=in_, **kw)
        return self.op(eng, fn, waits, sem, 16, chain=False)

    def wait(self, eng, *toks):
        toks = tuple(t for t in toks if t is not None)

        def fn(e):
            return e.nop()
        self.steps[eng].append((toks, fn, ()))

    def realize(self, block):
        for eng_name in ENGINES:
            steps = self.steps[eng_name]

            def mk(steps):
                def run(eng):
                    for waits, fn, incs in steps:
                        for s, v in waits:
                            eng.wait_ge(s.h, v)
                        inst = fn(eng)
                        for s, amt in incs:
                            inst.then_inc(s.h, amt)
                return run
            getattr(block, eng_name)(mk(steps))


def build(stage):
    nc = bass.Bass("TRN2", target_bir_lowering=False)
    p = Prog(nc)

    def di(name, shape, dt=F32):
        return nc.dram_tensor(name, shape, dt, kind="ExternalInput").ap()

    x_h = di("x", [T, D])
    wk_h = di("wk", [D, D]); wv_h = di("wv", [D, D])
    wr_h = di("wr", [D, D]); wo_h = di("wo", [D, D])
    wfk_h = di("wfk", [D, DFF]); wfv_h = di("wfv", [DFF, D]); wfr_h = di("wfr", [D, D])
    sg_h = di("sg", [D, DSH]); su_h = di("su", [D, DSH]); sd_h = di("sd", [DSH, D])
    rg_h = di("rg", [D, DFF]); ru_h = di("ru", [D, DFF]); rd_h = di("rd", [DFF, D])
    rwt_h = di("rwt", [D, E])
    vecs_h = di("vecs", [128, FC, 9])
    aux_h = di("aux", [128, 8])
    aux8_h = di("aux8", [8, 8])
    previdx_h = di("previdx", [128, FC], U32)
    previdx2_h = di("previdx2", [128, FC, 2], U32)
    id32_h = di("ident32", [128, 128])
    id8_h = di("ident8", [8, 8])
    idbf_h = di("identbf", [128, 128], BF16)

    out_h = nc.dram_tensor("out", [T, D], F32, kind="ExternalOutput").ap()

    def dd(name, shape, dt=F32):
        return nc.dram_tensor(name, shape, dt).ap()

    xnfm_d = dd("xnfm_d", [FC, 128, T])
    xnbf_d = dd("xnbf_d", [FC, 128, T], BF16)
    ag1_in = dd("ag1_in", [T, D], BF16)
    ag1_out = dd("ag1_out", [NCORE * T, D], BF16)
    r1a_in = dd("r1a_in", [1, D]); r1a_out = dd("r1a_out", [NCORE, D])
    r1b_in = dd("r1b_in", [1, 2 * D]); r1b_out = dd("r1b_out", [NCORE, 2 * D])
    r2_in = dd("r2_in", [1, D]); r2_out = dd("r2_out", [NCORE, D])
    cnt_in = dd("cnt_in", [1, E]); cnt_out = dd("cnt_out", [NCORE, E])
    tbl_in = dd("tbl_in", [E * CAP, 1]); tbl_out = dd("tbl_out", [E * CAP, 1])
    A_d = dd("A_d", [FC, 128, T]); B_d = dd("B_d", [FC, 128, T]); RS_d = dd("RS_d", [FC, 128, T])
    x2_d = dd("x2_d", [FC, 128, T])
    stm_d = dd("stm_d", [T, D])
    myrow_d = dd("myrow_d", [CAP, 1])
    ag4_in = dd("ag4_in", [CAP, D]); ag4_out = dd("ag4_out", [NCORE * CAP, D])
    shd_d = dd("shd_d", [FC, 128, T])

    with ExitStack() as ctx:
        sems = {}
        for name in ("LD", "SP", "DV", "ACT", "PE", "PL", "CC", "ST"):
            sems[name] = Sem(ctx.enter_context(nc.semaphore(name.lower())), name)
        LD, SP, DV, ACT_S, PE, PL, CC, ST = (sems[k] for k in
                                             ("LD", "SP", "DV", "ACT", "PE", "PL", "CC", "ST"))
        for eng_name in ("vector", "scalar"):
            p.auto[eng_name] = Sem(ctx.enter_context(nc.semaphore("au_" + eng_name)),
                                   "AU_" + eng_name)
        AUPL = Sem(ctx.enter_context(nc.semaphore("au_pool")), "AU_PL")
        p._mkstream = lambda name: Sem(ctx.enter_context(nc.semaphore("st_" + name)), name)

        sb = lambda n, s, d=F32: ctx.enter_context(nc.sbuf_tensor(n, s, d))

        id32 = sb("id32", [128, 128])
        id8 = sb("id8", [8, 8])
        idbf = sb("idbf", [128, 128], BF16)
        vecs = sb("s_vecs", [128, FC, 9])
        dcols = sb("dcols", [128, FC, 8])
        auxc = sb("auxc", [128, 8])
        aux8 = sb("s_aux8", [8, 8])
        pidx = sb("pidx", [128, FC], U32)
        pidx2 = sb("pidx2", [128, FC, 2], U32)
        W1 = sb("W1", [128, NT]); W2 = sb("W2", [128, NT])
        IDX1 = sb("IDX1", [128, NT], U32); IDX2 = sb("IDX2", [128, NT], U32)
        COLS = sb("COLS", [128, 64])
        CARR = sb("CARR", [128, FC, 4])
        pmid = ExitStack()
        ramp = pmid.enter_context(nc.sbuf_tensor("ramp", [128, T], F32))
        XN1 = pmid.enter_context(nc.sbuf_tensor("XN1", [128, FC, 8 + T], BF16))

        f0c = auxc[:, 0:1]
        f1c = auxc[:, 1:2]

        def dv(fn, waits=(), sem=None):
            return p.op("vector", fn, waits, sem)

        def ac(fn, waits=(), sem=None):
            return p.op("scalar", fn, waits, sem)

        def te(fn, waits=(), sem=None):
            return p.op("tensor", fn, waits, sem)

        def gp(fn, waits=(), sem=None):
            p.auto["gpsimd"] = AUPL
            r = p.op("gpsimd", fn, waits, sem)
            del p.auto["gpsimd"]
            return r

        def combine2(HMv, mdst, vdst, wait):
            w = HMv.ap().shape[1] if hasattr(HMv, 'ap') else HMv.shape[1]
            a = COLS[:, 32:32 + w]; b2 = COLS[:, 48:48 + w]
            dv(lambda e: e.tensor_tensor(mdst, HMv[:, :, 0], HMv[:, :, 2], ALU.add), waits=[wait])
            dv(lambda e: e.tensor_scalar(mdst, mdst, 0.5, None, ALU.mult))
            dv(lambda e: e.tensor_tensor(a, HMv[:, :, 0], HMv[:, :, 0], ALU.mult))
            dv(lambda e: e.tensor_tensor(a, a, HMv[:, :, 1], ALU.add))
            dv(lambda e: e.tensor_tensor(b2, HMv[:, :, 2], HMv[:, :, 2], ALU.mult))
            dv(lambda e: e.tensor_tensor(b2, b2, HMv[:, :, 3], ALU.add))
            dv(lambda e: e.tensor_tensor(a, a, b2, ALU.add))
            dv(lambda e: e.tensor_scalar(a, a, 0.5, None, ALU.mult))
            dv(lambda e: e.tensor_tensor(b2, mdst, mdst, ALU.mult))
            return dv(lambda e: e.tensor_tensor(vdst, a, b2, ALU.subtract), sem=DV)

        def combine_halves(mdst, vdst, wait):
            a = COLS[:, 0:NT]; b2 = COLS[:, 16:32]
            t0 = dv(lambda e: e.tensor_tensor(mdst, HM[:, :, 0], HM[:, :, 2], ALU.add), waits=[wait])
            dv(lambda e: e.tensor_scalar(mdst, mdst, 0.5, None, ALU.mult))
            dv(lambda e: e.tensor_tensor(a, HM[:, :, 0], HM[:, :, 0], ALU.mult))
            dv(lambda e: e.tensor_tensor(a, a, HM[:, :, 1], ALU.add))
            dv(lambda e: e.tensor_tensor(b2, HM[:, :, 2], HM[:, :, 2], ALU.mult))
            dv(lambda e: e.tensor_tensor(b2, b2, HM[:, :, 3], ALU.add))
            dv(lambda e: e.tensor_tensor(a, a, b2, ALU.add))
            dv(lambda e: e.tensor_scalar(a, a, 0.5, None, ALU.mult))
            dv(lambda e: e.tensor_tensor(b2, mdst, mdst, ALU.mult))
            return dv(lambda e: e.tensor_tensor(vdst, a, b2, ALU.subtract), sem=DV)

        def rstd_batch(dstv, srcv, scale, eps, wait):
            t0 = dv(lambda e: e.tensor_scalar(dstv, srcv, scale, eps, ALU.mult, ALU.add),
                    waits=[wait], sem=DV)
            t1 = ac(lambda e: e.activation(dstv, dstv, AF.Ln), waits=[t0], sem=ACT_S)
            t2 = dv(lambda e: e.tensor_scalar(dstv, dstv, -0.5, None, ALU.mult), waits=[t1], sem=DV)
            return ac(lambda e: e.activation(dstv, dstv, AF.Exp), waits=[t2], sem=ACT_S)

        p01 = ExitStack()
        sbp = lambda n, sh, d=F32: p01.enter_context(nc.sbuf_tensor(n, sh, d))
        rwtc = sbp("rwtc", [128, FC, E])
        LG = sbp("LG", [128, NT, E])
        EQ1 = sbp("EQ1", [128, NT, E]); EQ2 = sbp("EQ2", [128, NT, E])
        MX = sbp("MX", [128, NT, 4])
        KP1 = sbp("KP1", [128, NT]); KP2 = sbp("KP2", [128, NT])
        SLT1 = sbp("SLT1", [128, NT]); SLT2 = sbp("SLT2", [128, NT])
        STATC = sbp("STATC", [128, NT, 6])
        HM = sbp("HM", [128, NT, 4])
        RMSC = sbp("RMSC", [128, NT, 2])
        TM = p01.enter_context(nc.sbuf_tensor("TM", [128, NT, D], F32))
        SQS = p01.enter_context(nc.sbuf_tensor("SQS", [128, D], F32))
        Z1 = p01.enter_context(nc.sbuf_tensor("Z1", [128, D], F32))
        STG = p01.enter_context(nc.sbuf_tensor("STG", [128, D], F32))
        ST6 = p01.enter_context(nc.sbuf_tensor("ST6", [128, 12], F32))
        p0ctx = ExitStack()
        PT0 = p0ctx.enter_context(nc.psum_tensor("PT0", [128, 512], F32))
        PT1 = p0ctx.enter_context(nc.psum_tensor("PT1", [128, 512], F32))
        PT2 = p0ctx.enter_context(nc.psum_tensor("PT2", [128, 512], F32))
        PT3 = p0ctx.enter_context(nc.psum_tensor("PT3", [128, 512], F32))
        PLG = p0ctx.enter_context(nc.psum_tensor("PLG", [128, NT * E], F32))

        t_ldx = p.dma("sync", TM[:, :, :],
                      x_h.rearrange("(i p) f -> p i f", p=128), sem=LD, stream="d0")
        for dst, src in ((id32[:, :], id32_h[:, :]), (id8[:, :], id8_h[:, :]),
                         (idbf[:, :], idbf_h[:, :]), (vecs[:, :, :], vecs_h),
                         (auxc[:, :], aux_h[:, :]), (aux8[:, :], aux8_h[:, :]),
                         (pidx[:, :], previdx_h[:, :]),
                         (pidx2[:, :, :], previdx2_h),
                         (rwtc[:, :, :], rwt_h.rearrange("(fc p) e -> p fc e", p=128))):
            t_lds = p.dma("sync", dst, src, sem=LD, stream="d1")

        t_ramp = gp(lambda e: e.iota(ramp[:, :], [[1, T]], base=1, channel_multiplier=0,
                                     allow_small_or_imprecise_dtypes=True), sem=PL)

        dv(lambda e: e.tensor_scalar(dcols[:, :, 0], vecs[:, :, 4], -1.0, 1.0, ALU.mult, ALU.add), waits=[t_lds])
        dv(lambda e: e.tensor_scalar(dcols[:, :, 1], vecs[:, :, 5], -1.0, 1.0, ALU.mult, ALU.add))
        dv(lambda e: e.tensor_scalar(dcols[:, :, 2], vecs[:, :, 6], -1.0, 1.0, ALU.mult, ALU.add))
        t_dvmix = dv(lambda e: e.tensor_scalar(dcols[:, :, 6], vecs[:, :, 8], f0c, None, ALU.mult), sem=DV)
        t_w = ac(lambda e: e.activation(dcols[:, :, 3], vecs[:, :, 7], AF.Exp), waits=[t_lds], sem=ACT_S)
        t_negw = dv(lambda e: e.tensor_scalar(dcols[:, :, 4], dcols[:, :, 3], -1.0, None, ALU.mult),
                    waits=[t_w], sem=DV)
        t_ew = ac(lambda e: e.activation(dcols[:, :, 5], dcols[:, :, 4], AF.Exp), waits=[t_negw], sem=ACT_S)

        for i in range(NT):
            tile = TM[:, i, :]
            dv(lambda e, t=tile: e.bn_stats(ST6[:, 0:6], t[:, 0:512]), waits=[t_ldx, t_lds, t_ramp, t_ew, t_dvmix] if i == 0 else ())
            dv(lambda e, t=tile: e.bn_stats(ST6[:, 6:12], t[:, 512:1024]))
            dv(lambda e, i=i: e.bn_aggr(HM[:, i, 0:2], ST6[:, 0:6]))
            last = dv(lambda e, i=i: e.bn_aggr(HM[:, i, 2:4], ST6[:, 6:12]),
                      sem=DV if i == NT - 1 else None)
        t_stats = last
        t_stats = combine_halves(STATC[:, :, 0], STATC[:, :, 1], t_stats)

        t_rstd0 = rstd_batch(STATC[:, :, 2], STATC[:, :, 1], 1.0, LN_EPS, t_stats)

        for i in range(NT):
            tile = TM[:, i, :]
            dv(lambda e, t=tile, i=i: e.tensor_scalar(
                t, t, STATC[:, i, 0:1], STATC[:, i, 2:3], ALU.subtract, ALU.mult),
                waits=[t_rstd0] if i == 0 else ())
            dv(lambda e, t=tile: e.bn_stats(ST6[:, 0:6], t[:, 0:512]))
            dv(lambda e, t=tile: e.bn_stats(ST6[:, 6:12], t[:, 512:1024]))
            dv(lambda e, i=i: e.bn_aggr(HM[:, i, 0:2], ST6[:, 0:6]))
            t_xn = dv(lambda e, i=i: e.bn_aggr(HM[:, i, 2:4], ST6[:, 6:12]),
                      sem=DV if i == NT - 1 else None)

        for i in range(NT):
            tile = TM[:, i, :]
            t_rms = ac(lambda e, t=tile, i=i: e.activation(SQS[:, :], t, AF.Square,
                                                           accum_out=RMSC[:, i, 0:1]),
                       waits=[t_xn] if i == 0 else (), sem=ACT_S if i == NT - 1 else None)
        t_rrstd = rstd_batch(RMSC[:, :, 1], RMSC[:, :, 0], 1.0 / D, RMS_EPS, t_rms)

        t_ship = p.dma("gpsimd", ag1_in.rearrange("(i p) f -> p i f", p=128),
                       TM[:, :, :], waits=[t_xn], sem=SP, stream="d2")
        t_ag1 = p.op("gpsimd", chain=False, fn=lambda e: e.collective_compute(
            "AllGather", ALU.bypass, replica_groups=[list(range(NCORE))],
            ins=[ag1_in[:, :]], outs=[ag1_out[:, :]]), waits=[t_ship], sem=CC)

        t_m1v1 = combine_halves(STATC[:, :, 3], STATC[:, :, 4], t_xn)
        t_rstd1 = rstd_batch(STATC[:, :, 5], STATC[:, :, 4], 1.0, LN_EPS, t_m1v1)
        t_pad = dv(lambda e: e.memset(XN1[:, :, 0:8], 0.0), sem=DV)

        t_evx = None
        t_evz = None
        t_lgt = None
        t_sp = None
        t_sp3 = None
        t_ztr = None
        for i in range(NT):
            tile = TM[:, i, :]
            tz = dv(lambda e, t=tile, i=i: e.tensor_scalar(
                Z1[:, :], t, STATC[:, i, 3:4], STATC[:, i, 5:6], ALU.subtract, ALU.mult),
                waits=[t_rstd1 if i == 0 else None, t_ztr], sem=DV)
            for fc in range(FC):
                pt = PT0 if fc < 4 else PT1
                te(lambda e, t=tile, fc=fc, pt=pt: e.transpose(
                    pt[:, (fc % 4) * 128:(fc % 4) * 128 + 128],
                    t[:, fc * 128:fc * 128 + 128], id32[:, :]),
                    waits=(tz, t_evx, t_evz) if fc == 0 else ())
            for fc in range(FC):
                pt = PT2 if fc < 4 else PT3
                t_ztr = te(lambda e, fc=fc, pt=pt: e.transpose(
                    pt[:, (fc % 4) * 128:(fc % 4) * 128 + 128],
                    Z1[:, fc * 128:fc * 128 + 128], id32[:, :]),
                    waits=[tz] if fc == 0 else (), sem=PE if fc == FC - 1 else None)
            ac(lambda e: e.copy(STG[:, 0:512], PT0[:, :]), waits=(t_ztr, t_sp3, t_sp, t_lgt))
            t_evx = ac(lambda e: e.copy(STG[:, 512:1024], PT1[:, :]), sem=ACT_S)
            t_sp3 = p.dma("sync", xnfm_d[:, :, i * 128:(i + 1) * 128].rearrange("fc p t -> p fc t"),
                  STG.ap().rearrange("p (fc t) -> p fc t", fc=FC), waits=[t_evx], sem=SP, stream="d3")
            t_sp = p.dma("gpsimd", xnbf_d[:, :, i * 128:(i + 1) * 128].rearrange("fc p t -> p fc t"),
                         STG.ap().rearrange("p (fc t) -> p fc t", fc=FC), waits=[t_evx], sem=SP, stream="d4")
            for fc in range(FC):
                t_lgt = te(lambda e, fc=fc, i=i: e.matmul(
                    PLG[:, i * E:(i + 1) * E], STG[:, fc * 128:(fc + 1) * 128],
                    rwtc[:, fc, :], start=(fc == 0), stop=(fc == FC - 1)),
                    waits=[t_evx, t_lds] if fc == 0 else (), sem=PE if fc == FC - 1 else None)
            for fc in range(FC):
                pt = PT2 if fc < 4 else PT3
                t_evz = dv(lambda e, fc=fc, pt=pt, i=i: e.tensor_scalar(
                    XN1[:, fc, 8 + i * 128:8 + (i + 1) * 128],
                    pt[:, (fc % 4) * 128:(fc % 4) * 128 + 128],
                    vecs[:, fc, 0:1], vecs[:, fc, 1:2], ALU.mult, ALU.add),
                    waits=(t_ztr, t_pad) if fc == 0 else (), sem=DV if fc == FC - 1 else None)
        t_xn1 = t_evz

        t_lg = None
        for i in range(NT):
            t_lg = dv(lambda e, i=i: e.tensor_scalar(
                LG[:, i, :], PLG[:, i * E:(i + 1) * E], RMSC[:, i, 1:2], None, ALU.mult),
                waits=(t_lgt, t_rrstd) if i == 0 else (), sem=DV if i == NT - 1 else None)

        PR1 = p0ctx.enter_context(nc.psum_tensor("PR1", [1, 1024], BF16))
        SR1 = p01.enter_context(nc.sbuf_tensor("SR1", [1, 1024], F32))
        PRV = p01.enter_context(nc.sbuf_tensor("PRV", [128, FC], F32))
        for fc in range(FC):
            t_tr1 = te(lambda e, fc=fc: e.transpose(
                PR1[:, fc * 128:(fc + 1) * 128], XN1[:, fc, 7 + T:8 + T], idbf[:, :]),
                waits=[t_xn1] if fc == 0 else (), sem=PE if fc == FC - 1 else None)
        t_evr1 = dv(lambda e: e.tensor_copy(SR1[:, :], PR1[:, :]), waits=[t_tr1], sem=DV)
        t_pub1 = p.dma("sync", r1a_in[:, :], SR1[:, :], waits=[t_evr1], sem=SP, stream="d5")
        t_r1a = p.op("gpsimd", lambda e: e.collective_compute(
            "AllGather", ALU.bypass, replica_groups=[list(range(NCORE))],
            ins=[r1a_in[:, :]], outs=[r1a_out[:, :]]), waits=[t_pub1], sem=CC)
        for fc in range(FC):
            t_prv = p.op("gpsimd", lambda e, fc=fc: e.indirect_dma_start(
                out=PRV[:, fc:fc + 1], out_offset=None,
                in_=r1a_out.rearrange("c (d o) -> (c d) o", o=1),
                in_offset=IndirectOffsetOnAxis(ap=pidx[:, fc:fc + 1], axis=0)),
                waits=[t_r1a] if fc == 0 else (), sem=p.ssem("ind1"), amt=16)
        for fc in range(FC):
            t_prvw = dv(lambda e, fc=fc: e.tensor_scalar(
                XN1[:, fc, 7:8], PRV[:, fc:fc + 1], f1c, None, ALU.mult),
                waits=[t_prv] if fc == 0 else (), sem=DV if fc == FC - 1 else None)

        p0ctx.close()
        SEL8 = p01.enter_context(nc.sbuf_tensor("SEL8", [8, T], F32))
        CUM8 = p01.enter_context(nc.sbuf_tensor("CUM8", [8, T], F32))
        Q8 = p01.enter_context(nc.sbuf_tensor("Q8", [8, T], F32))
        KP8 = p01.enter_context(nc.sbuf_tensor("KP8", [8, T], F32))
        SL8 = p01.enter_context(nc.sbuf_tensor("SL8", [8, T], F32))
        EQ8 = p01.enter_context(nc.sbuf_tensor("EQ8", [8, 2 * T], F32))
        ON8 = p01.enter_context(nc.sbuf_tensor("ON8", [8, T], F32))
        SC8 = p01.enter_context(nc.sbuf_tensor("SC8", [8, 64], F32))
        TKID = p01.enter_context(nc.sbuf_tensor("TKID", [128, NT], F32))
        SID1 = p01.enter_context(nc.sbuf_tensor("SID1", [128, NT], U32))
        SID2 = p01.enter_context(nc.sbuf_tensor("SID2", [128, NT], U32))
        TMPT = p01.enter_context(nc.sbuf_tensor("TMPT", [128, NT, E], F32))
        ZROW = p01.enter_context(nc.sbuf_tensor("ZROW", [128, E * CAP // 128], F32))
        P8A = p01.enter_context(nc.psum_tensor("P8A", [8, T], F32))
        P8B = p01.enter_context(nc.psum_tensor("P8B", [128, NT * E], F32))

        t_m1 = dv(lambda e: e.tensor_reduce(MX[:, :, 0], LG[:, :, :], AX.X, ALU.max),
                  waits=[t_lg], sem=DV)
        for i in range(NT):
            dv(lambda e, i=i: e.tensor_scalar(EQ1[:, i, :], LG[:, i, :], MX[:, i, 0:1],
                                              None, ALU.is_equal),
               waits=[t_m1] if i == 0 else ())
        t_eq1 = dv(lambda e: e.scalar_tensor_tensor(TMPT[:, :, :], EQ1[:, :, :], -BIG,
                                                    LG[:, :, :], ALU.mult, ALU.add), sem=DV)
        t_m2 = dv(lambda e: e.tensor_reduce(MX[:, :, 1], TMPT[:, :, :], AX.X, ALU.max),
                  waits=[t_eq1], sem=DV)
        for i in range(NT):
            t_eq2 = dv(lambda e, i=i: e.tensor_scalar(EQ2[:, i, :], TMPT[:, i, :], MX[:, i, 1:2],
                                                      None, ALU.is_equal),
                       waits=[t_m2] if i == 0 else (), sem=DV if i == NT - 1 else None)
        dv(lambda e: e.tensor_tensor(MX[:, :, 2], MX[:, :, 0], MX[:, :, 1], ALU.subtract),
           waits=[t_m2])
        t_d = dv(lambda e: e.tensor_scalar(MX[:, :, 3], MX[:, :, 2], -1.0, None, ALU.mult), sem=DV)
        t_rw = ac(lambda e: e.activation(MX[:, :, 2:4], MX[:, :, 2:4], AF.Sigmoid),
                  waits=[t_d], sem=ACT_S)

        t_sel = dv(lambda e: e.tensor_tensor(TMPT[:, :, :], EQ1[:, :, :], EQ2[:, :, :], ALU.add),
                   waits=[t_eq2], sem=DV)
        for i in range(NT):
            t_trs = te(lambda e, i=i: e.transpose(
                P8A[:, i * 128:(i + 1) * 128], TMPT[:, i, :], id32[:, :]),
                waits=[t_sel] if i == 0 else (), sem=PE if i == NT - 1 else None)
        t_sel8 = dv(lambda e: e.tensor_copy(SEL8[:, :], P8A[:, :]), waits=[t_trs], sem=DV)
        dv(lambda e: e.memset(ON8[:, :], 1.0))
        t_cum = dv(lambda e: e.tensor_tensor_scan(CUM8[:, :], ON8[:, :], SEL8[:, :],
                                                  0.0, ALU.mult, ALU.add), sem=DV)
        t_cnt_tr = te(lambda e: e.transpose(P8A[0:1, 0:8], CUM8[:, T - 1:T], id8[:, :]),
                      waits=[t_cum], sem=PE)
        t_cnt = dv(lambda e: e.tensor_copy(SC8[0:1, 0:8], P8A[0:1, 0:8]), waits=[t_cnt_tr], sem=DV)
        t_pubc = p.dma("sync", cnt_in[:, :], SC8[0:1, 0:8], waits=[t_cnt], sem=SP, stream="d6")
        t_agc = p.op("gpsimd", lambda e: e.collective_compute(
            "AllGather", ALU.bypass, replica_groups=[list(range(NCORE))],
            ins=[cnt_in[:, :]], outs=[cnt_out[:, :]]), waits=[t_pubc], sem=CC)
        t_ldc = p.dma("sync", SC8[:, 8:16], cnt_out[:, :], waits=[t_agc], sem=LD, stream="d7")
        t_msk = dv(lambda e: e.tensor_scalar(SC8[:, 16:24], SC8[:, 8:16], aux8[:, 0:1],
                                             None, ALU.mult), waits=[t_ldc], sem=DV)
        t_otr = te(lambda e: e.transpose(P8A[0:8, 8:16], SC8[:, 16:24], id8[:, :]),
                   waits=[t_msk], sem=PE)
        t_offs = dv(lambda e: e.tensor_reduce(SC8[:, 24:25], P8A[:, 8:16], AX.X, ALU.add),
                    waits=[t_otr], sem=DV)

        dv(lambda e: e.tensor_tensor(Q8[:, :], CUM8[:, :], SEL8[:, :], ALU.subtract),
           waits=[t_offs])
        dv(lambda e: e.tensor_scalar(Q8[:, :], Q8[:, :], SC8[:, 24:25], None, ALU.add))
        dv(lambda e: e.tensor_scalar(KP8[:, :], Q8[:, :], float(CAP), None, ALU.is_lt))
        dv(lambda e: e.tensor_tensor(KP8[:, :], KP8[:, :], SEL8[:, :], ALU.mult))
        t_slot = dv(lambda e: e.tensor_scalar(SL8[:, :], Q8[:, :], aux8[:, 2:3], None, ALU.add),
                    sem=DV)

        PR8 = p01.enter_context(nc.sbuf_tensor("PR8", [8, T], F32))
        t_eqem = t_slot
        for k, (eqsrc, dstoff) in enumerate(((EQ1, 0), (EQ2, T))):
            for i in range(NT):
                t_tre = te(lambda e, i=i, eqsrc=eqsrc: e.transpose(
                    P8A[:, i * 128:(i + 1) * 128], eqsrc[:, i, :], id32[:, :]),
                    waits=[t_eqem] if i == 0 else (), sem=PE if i == NT - 1 else None)
            t_eqem = dv(lambda e, dstoff=dstoff: e.tensor_copy(
                EQ8[:, dstoff:dstoff + T], P8A[:, :]), waits=[t_tre], sem=DV)

        def pull_tm(eq_off, Xsrc, dst, wait_tok):
            t_pr = dv(lambda e: e.tensor_tensor(PR8[:, :], EQ8[:, eq_off:eq_off + T],
                                                Xsrc[:, :], ALU.mult),
                      waits=[wait_tok], sem=DV)
            for i in range(NT):
                t_trp = te(lambda e, i=i: e.transpose(
                    P8B[:, i * 8:(i + 1) * 8], PR8[:, i * 128:(i + 1) * 128], id8[:, :]),
                    waits=[t_pr] if i == 0 else (), sem=PE if i == NT - 1 else None)
            return dv(lambda e: e.tensor_reduce(
                dst, P8B.ap().rearrange("p (i e) -> p i e", e=E), AX.X, ALU.add),
                waits=[t_trp], sem=DV)

        t_k1 = pull_tm(0, KP8, KP1[:, :], t_eqem)
        t_k2 = pull_tm(T, KP8, KP2[:, :], t_k1)
        t_s1 = pull_tm(0, SL8, SLT1[:, :], t_k2)
        t_s2 = pull_tm(T, SL8, SLT2[:, :], t_s1)

        dv(lambda e: e.tensor_tensor(W1[:, :], MX[:, :, 2], KP1[:, :], ALU.mult), waits=[t_s2, t_rw])
        dv(lambda e: e.tensor_tensor(W2[:, :], MX[:, :, 3], KP2[:, :], ALU.mult))
        dv(lambda e: e.tensor_tensor(SQS[:, 16:32], SLT1[:, :], KP1[:, :], ALU.mult))
        dv(lambda e: e.tensor_copy(IDX1[:, :], SQS[:, 16:32]))
        dv(lambda e: e.tensor_tensor(SQS[:, 16:32], SLT2[:, :], KP2[:, :], ALU.mult))
        dv(lambda e: e.tensor_copy(IDX2[:, :], SQS[:, 16:32]))
        dv(lambda e: e.tensor_scalar(SQS[:, 0:NT], KP1[:, :], -BIG, BIG, ALU.mult, ALU.add))
        dv(lambda e: e.tensor_tensor(SQS[:, 0:NT], SQS[:, 0:NT], SLT1[:, :], ALU.add))
        dv(lambda e: e.tensor_copy(SID1[:, :], SQS[:, 0:NT]))
        dv(lambda e: e.tensor_scalar(SQS[:, 0:NT], KP2[:, :], -BIG, BIG, ALU.mult, ALU.add))
        dv(lambda e: e.tensor_tensor(SQS[:, 0:NT], SQS[:, 0:NT], SLT2[:, :], ALU.add))
        t_sidx = dv(lambda e: e.tensor_copy(SID2[:, :], SQS[:, 0:NT]), sem=DV)

        t_io = gp(lambda e: e.iota(TKID[:, :], [[128, NT]], base=0, channel_multiplier=1,
                                   allow_small_or_imprecise_dtypes=True), sem=PL)
        t_tk = dv(lambda e: e.tensor_scalar(TKID[:, :], TKID[:, :], auxc[:, 2:3], None,
                                            ALU.add), waits=[t_io], sem=DV)
        t_zr = dv(lambda e: e.memset(ZROW[:, :], 0.0), sem=DV)
        t_ztb = p.dma("sync", tbl_in.rearrange("(px x) o -> px (x o)", px=128),
                      ZROW[:, 0:E * CAP // 128], waits=[t_zr], sem=SP, stream="d8")
        t_sc = None
        for sid in (SID1, SID2):
            for i in range(NT):
                t_sc = p.op("gpsimd", lambda e, sid=sid, i=i: e.indirect_dma_start(
                    out=tbl_in[:, :],
                    out_offset=IndirectOffsetOnAxis(ap=sid[:, i:i + 1], axis=0),
                    in_=TKID[:, i:i + 1], in_offset=None,
                    bounds_check=E * CAP - 1, oob_is_err=False),
                    waits=(t_sidx, t_tk, t_ztb) if (sid is SID1 and i == 0) else (),
                    sem=p.ssem("ind2"), amt=16)
        t_ar = p.op("gpsimd", lambda e: e.collective_compute(
            "AllReduce", ALU.add, replica_groups=[list(range(NCORE))],
            ins=[tbl_in[:, :]], outs=[tbl_out[:, :]]), waits=[t_sc], sem=CC)

        if stage == "stats":
            t0 = p.dma("sync", out_h[16:28, :].rearrange("(o q) d -> o (q d)", o=1).rearrange("o (q x) -> q (o x)", q=128),
                       STATC[:, :, :], waits=[t_rstd0], sem=ST, stream="d9")
            t0b = p.dma("sync", out_h[28:40, :].rearrange("(o q) d -> o (q d)", o=1).rearrange("o (q x) -> q (o x)", q=128),
                       STATC[:, :, :], waits=[t_xn], sem=ST, stream="d10")
            t1 = p.dma("sync", out_h[0:12, :].rearrange("(o q) d -> o (q d)", o=1).rearrange("o (q x) -> q (o x)", q=128),
                       STATC[:, :, :], waits=[t_rstd1], sem=ST, stream="d11")
            t2 = p.dma("sync", out_h[12:16, :].rearrange("(o q) d -> o (q d)", o=1).rearrange("o (q x) -> q (o x)", q=128),
                       RMSC[:, :, :], waits=[t_rrstd], sem=ST, stream="d12")
            p.wait("sync", t1, t2)
        elif stage == "xn":
            tend = p.dma("sync", out_h.rearrange("(o a) b -> o (a b)", o=1),
                         xnfm_d.rearrange("(o f) p t -> o (f p t)", o=1), waits=[p.ssem("d3").tok(), p.ssem("d4").tok()], sem=ST, stream="d13")
            p.wait("sync", tend)
        elif stage == "xn1":
            tend = p.dma("gpsimd", out_h.rearrange("(p r) d -> p (r d)", p=128),
                         XN1[:, :, 8:8 + T], waits=[t_xn1, t_prvw], sem=ST, stream="d14")
            p.wait("gpsimd", tend)
        elif stage == "router":
            toks = [t_ar]
            row = 0

            def dump(src_ap, waits):
                nonlocal row
                if len(src_ap.shape) == 3:
                    src_ap = src_ap.rearrange("p a b -> p (a b)")
                P_, rest = src_ap.shape[0], src_ap.shape[1]
                total = P_ * rest
                nrow = (total + 1023) // 1024
                if total % 1024 == 0:
                    dst = out_h[row:row + nrow, :]
                else:
                    dst = out_h[row:row + 1, 0:total]
                dst = dst.rearrange("(o a) b -> o (a b)", o=1).rearrange("o (q x) -> q (o x)", q=P_)
                tt = p.dma("sync", dst, src_ap, waits=waits, sem=ST, stream="d15")
                row += nrow
                toks.append(tt)
            dump(LG.ap(), [t_lg])
            dump(MX.ap(), [t_rw])
            for t8, tok in ((SEL8, t_eqem), (CUM8, t_cum), (Q8, t_slot),
                            (KP8, t_slot), (SL8, t_slot)):
                dump(t8.ap(), [tok])
            for ttm, tok in ((KP1, t_k1), (KP2, t_k2), (SLT1, t_s1), (SLT2, t_s2),
                             (W1, t_sidx), (W2, t_sidx), (TKID, t_tk)):
                dump(ttm.ap(), [tok])
            dump(tbl_out.rearrange("(r d) o -> r (d o)", d=1024), [t_ar])
            nonlocal_row = row
            tt1 = p.dma("gpsimd", out_h[nonlocal_row:nonlocal_row + 2, :].rearrange(
                "(o a) b -> o (a b)", o=1).rearrange("o (q x) -> q (o x)", q=128),
                SID1[:, :], waits=[t_sidx], sem=ST, stream="d16")
            row += 2
            toks.append(tt1)
            dump(TKID.ap(), [t_tk])
            dump(SC8[:, 0:32], [t_offs])
            p.wait("sync", *toks)
        elif stage in ("s", "wkv", "x2", "yn", "shared", "xbf", "hb", "full"):
            TTW = 512
            NTT = T // TTW

            def gate():
                return [(p.auto["vector"], p.auto["vector"].n),
                        (p.auto["scalar"], p.auto["scalar"].n),
                        PE.tok(), PL.tok()] + [sm.tok() for sm in p.streams.values()]

            g0 = gate()
            p01.close()
            pA = ExitStack()
            sba = lambda n, sh, d=F32: pA.enter_context(nc.sbuf_tensor(n, sh, d))
            WKb = sba("WKb", [128, FC, D], BF16)
            WVb = sba("WVb", [128, FC, D], BF16)
            WRb = sba("WRb", [128, FC, D], BF16)
            MIXb = sba("MIXb", [128, FC, TTW], BF16)
            EKU = sba("EKU", [128, FC, TTW])
            ASC = sba("ASC", [128, FC, TTW])
            BSC = sba("BSC", [128, FC, TTW])
            RSS = sba("RSS", [128, FC, TTW])
            EWT_A = sba("EWT_A", [128, TTW])
            ONESW = sba("ONESW", [128, TTW])
            TMPA = sba("TMPA", [128, TTW])
            SR1b = sba("SR1b", [1, 2 * D])
            PRV2 = sba("PRV2", [128, FC, 2])
            PA = [pA.enter_context(nc.psum_tensor(f"PA{j}", [128, 512], F32))
                  for j in range(8)]

            t_wk = p.dma("gpsimd", WKb[:, :, :], wk_h.rearrange("(kc q) o -> q kc o", q=128),
                         waits=g0, sem=LD, stream="d17")
            t_wv = p.dma("gpsimd", WVb[:, :, :], wv_h.rearrange("(kc q) o -> q kc o", q=128), sem=LD, stream="d18")
            t_wr = p.dma("gpsimd", WRb[:, :, :], wr_h.rearrange("(kc q) o -> q kc o", q=128), sem=LD, stream="d19")

            dv(lambda e: e.memset(ONESW[:, :], 1.0), waits=g0)
            t_carr0 = dv(lambda e: e.memset(CARR[:, :, 0:2], 0.0), sem=DV)

            t_kpe = t_vpe = t_rpe = None
            t_ekev = t_uall = t_rsev = None
            t_spill = t_spA = t_spB = None
            for tt in range(NTT):
                cur = slice(8 + tt * TTW, 8 + (tt + 1) * TTW)
                prv = slice(7 + tt * TTW, 7 + (tt + 1) * TTW)

                def mixes(mixcol, omixcol, wait0, cur=cur, prv=prv):
                    tmix = None
                    for fc in range(FC):
                        dv(lambda e, fc=fc, prv=prv, omixcol=omixcol: e.tensor_scalar(
                            TMPA[:, :], XN1[:, fc, prv], dcols[:, fc, omixcol:omixcol + 1], None, ALU.mult),
                           waits=[wait0] if fc == 0 else ())
                        tmix = dv(lambda e, fc=fc, cur=cur, mixcol=mixcol: e.scalar_tensor_tensor(
                            MIXb[:, fc, :], XN1[:, fc, cur], vecs[:, fc, mixcol:mixcol + 1],
                            TMPA[:, :], ALU.mult, ALU.add), sem=DV)
                    return tmix

                t_mk = mixes(4, 0, t_rpe or t_carr0)
                for mc in range(FC):
                    for kc in range(FC):
                        t_kpe = te(lambda e, mc=mc, kc=kc: e.matmul(
                            PA[mc][:, :], WKb[:, kc, mc * 128:(mc + 1) * 128], MIXb[:, kc, :],
                            start=(kc == 0), stop=(kc == FC - 1)),
                            waits=[t_mk, t_wk, t_rsev] if (mc == 0 and kc == 0) else (),
                            sem=PE if kc == FC - 1 else None)
                t_ek_mc = []
                for mc in range(FC):
                    if tt == 0:
                        ac(lambda e, mc=mc: e.activation(
                            EKU[:, mc, 0:1], PA[mc][:, 0:1], AF.Exp, bias=dcols[:, mc, 6:7]),
                           waits=[t_kpe, t_uall] if mc == 0 else ())
                        t_ekev = ac(lambda e, mc=mc: e.activation(
                            EKU[:, mc, 1:TTW], PA[mc][:, 1:TTW], AF.Exp), sem=ACT_S)
                    else:
                        t_ekev = ac(lambda e, mc=mc: e.activation(
                            EKU[:, mc, :], PA[mc][:, :], AF.Exp),
                            waits=[t_kpe, t_uall] if mc == 0 else (), sem=ACT_S)
                    t_ek_mc.append(t_ekev)
                t_mv = mixes(5, 1, t_kpe)
                for mc in range(FC):
                    for kc in range(FC):
                        t_vpe = te(lambda e, mc=mc, kc=kc: e.matmul(
                            PA[mc][:, :], WVb[:, kc, mc * 128:(mc + 1) * 128], MIXb[:, kc, :],
                            start=(kc == 0), stop=(kc == FC - 1)),
                            waits=(([t_mv, t_wv] if mc == 0 else []) + [t_ek_mc[mc]])
                            if kc == 0 else (),
                            sem=PE if kc == FC - 1 else None)
                t_ufc = []
                for fc in range(FC):
                    dv(lambda e, fc=fc: e.tensor_scalar(
                        EWT_A[:, :], ONESW[:, :], dcols[:, fc, 5:6], None, ALU.mult),
                       waits=[t_vpe, t_spill, t_spA, t_spB] if fc == 0 else ())
                    dv(lambda e, fc=fc: e.tensor_tensor_scan(
                        BSC[:, fc, :], EWT_A[:, :], EKU[:, fc, :], CARR[:, fc, 1:2],
                        ALU.mult, ALU.add))
                    dv(lambda e, fc=fc: e.tensor_copy(CARR[:, fc, 1:2], BSC[:, fc, TTW - 1:TTW]))
                    dv(lambda e, fc=fc: e.tensor_scalar(
                        BSC[:, fc, :], BSC[:, fc, :], WKV_EPS, None, ALU.add))
                    t_ufc.append(dv(lambda e, fc=fc: e.tensor_tensor(
                        EKU[:, fc, :], EKU[:, fc, :], PA[fc][:, :], ALU.mult), sem=DV))
                t_mr = mixes(6, 2, t_vpe)
                for mc in range(FC):
                    for kc in range(FC):
                        t_rpe = te(lambda e, mc=mc, kc=kc: e.matmul(
                            PA[mc][:, :], WRb[:, kc, mc * 128:(mc + 1) * 128], MIXb[:, kc, :],
                            start=(kc == 0), stop=(kc == FC - 1)),
                            waits=[t_mr, t_wr, t_ufc[mc]] if kc == 0 else (),
                            sem=PE if kc == FC - 1 else None)
                for fc in range(FC):
                    dv(lambda e, fc=fc: e.tensor_scalar(
                        EWT_A[:, :], ONESW[:, :], dcols[:, fc, 5:6], None, ALU.mult))
                    dv(lambda e, fc=fc: e.tensor_tensor_scan(
                        ASC[:, fc, :], EWT_A[:, :], EKU[:, fc, :], CARR[:, fc, 0:1],
                        ALU.mult, ALU.add))
                    t_uall = dv(lambda e, fc=fc: e.tensor_copy(
                        CARR[:, fc, 0:1], ASC[:, fc, TTW - 1:TTW]), sem=DV)
                for mc in range(FC):
                    t_rsev = ac(lambda e, mc=mc: e.activation(
                        RSS[:, mc, :], PA[mc][:, :], AF.Sigmoid),
                        waits=[t_rpe, t_spill] if mc == 0 else (), sem=ACT_S)
                tsl = slice(tt * TTW, (tt + 1) * TTW)
                t_spA = p.dma("sync", A_d[:, :, tsl].rearrange("fc q t -> q fc t"), ASC[:, :, :],
                      waits=[t_uall], sem=SP, stream="d20")
                t_spB = p.dma("sync", B_d[:, :, tsl].rearrange("fc q t -> q fc t"), BSC[:, :, :],
                      waits=[t_uall], sem=SP, stream="d21")
                t_spill = p.dma("sync", RS_d[:, :, tsl].rearrange("fc q t -> q fc t"),
                                RSS[:, :, :], waits=[t_rsev], sem=SP, stream="d22")

            for fc in range(FC):
                j, off = divmod(fc * 128, 512)
                t_r1bt = te(lambda e, fc=fc, j=j, off=off: e.transpose(
                    PA[j][0:1, off:off + 128], CARR[:, fc, 0:1], id32[:, :]),
                    waits=[t_rsev, t_uall] if fc == 0 else (), sem=None)
            for fc in range(FC):
                j, off = divmod(1024 + fc * 128, 512)
                t_r1bt = te(lambda e, fc=fc, j=j, off=off: e.transpose(
                    PA[j][0:1, off:off + 128], CARR[:, fc, 1:2], id32[:, :]),
                    sem=PE if fc == FC - 1 else None)
            for j in range(4):
                t_evb = dv(lambda e, j=j: e.tensor_copy(
                    SR1b[:, j * 512:(j + 1) * 512], PA[j][0:1, :]),
                    waits=[t_r1bt] if j == 0 else (), sem=DV)
            t_pub2 = p.dma("sync", r1b_in[:, :], SR1b[:, :], waits=[t_evb], sem=SP, stream="d23")
            t_r1b = p.op("gpsimd", lambda e: e.collective_compute(
                "AllGather", ALU.bypass, replica_groups=[list(range(NCORE))],
                ins=[r1b_in[:, :]], outs=[r1b_out[:, :]]), waits=[t_pub2], sem=CC, chain=False)
            for fc in range(FC):
                for q in range(2):
                    t_prv2 = p.op("gpsimd", lambda e, fc=fc, q=q: e.indirect_dma_start(
                        out=PRV2[:, fc, q:q + 1], out_offset=None,
                        in_=r1b_out.rearrange("c (d o) -> (c d) o", o=1),
                        in_offset=IndirectOffsetOnAxis(ap=pidx2[:, fc, q:q + 1], axis=0)),
                        waits=[t_r1b] if (fc == 0 and q == 0) else (),
                        sem=p.ssem("ind3"), amt=16, chain=False)
            t_acar = None
            for fc in range(FC):
                t_acar = dv(lambda e, fc=fc: e.tensor_scalar(
                    CARR[:, fc, 2:4], PRV2[:, fc, 0:2], f1c, None, ALU.mult),
                    waits=[t_prv2] if fc == 0 else (), sem=DV)

            gB = gate()
            pA.close()
            pB = ExitStack()
            sbb = lambda n, sh, d=F32: pB.enter_context(nc.sbuf_tensor(n, sh, d))
            LAs = [sbb(f"LA{j}", [128, FC, TTW]) for j in range(2)]
            LBs = [sbb("LBs", [128, FC, TTW])] * 2
            LRs = [sbb("LRs", [128, FC, TTW])] * 2
            XNTs = [sbb(f"XNT{j}", [128, FC, TTW]) for j in range(2)]
            RWBs = [sbb("RWBs", [128, FC, TTW], BF16)] * 2
            X2Ss = [sbb(f"X2S{j}", [128, FC, TTW]) for j in range(1)] * 2
            X2Ft = [sbb(f"X2F{j}", [128, FC, 128]) for j in range(2)]
            Z2T = sbb("Z2T", [128, D])
            X2TMb = [Z2T] * 2
            WOb = sbb("WOb", [128, FC, D], BF16)
            EWTT = Z2T[:, 0:TTW]
            EWT2 = [Z2T[:, TTW:2 * TTW], sbb("EWT2b", [128, TTW])]
            ST6b = sbb("ST6b", [128, 12])
            LN2H = sbb("LN2H", [128, NT, 4])
            LN2S = sbb("LN2S", [128, NT, 3])
            SR2 = X2TMb[0][0:1, :]
            PRV3 = sbb("PRV3", [128, FC])
            t_bn2w = [None, None]
            PB = [pB.enter_context(nc.psum_tensor(f"PB{j}", [128, 512], F32))
                  for j in range(8)]

            t_wo = p.dma("gpsimd", WOb[:, :, :], wo_h.rearrange("(kc q) o -> q kc o", q=128),
                         waits=gB, sem=LD, stream="wo")
            t_yev = None
            t_wope = None
            t_x2 = None
            t_x2sp = None
            t_rw_tt = []
            t_ld_h = [None, None]
            t_xn_h = [None, None]
            t_pl_h = [None, None]
            _x2info = {}
            t_x2 = None
            t_x2sp = None

            def emit_x2evac(tt2):
                nonlocal t_x2, t_x2sp
                wopetok, tsl2 = _x2info[tt2]
                for mc in range(FC):
                    t_x2 = dv(lambda e, mc=mc, tt2=tt2: e.tensor_tensor(
                        X2Ss[tt2 % 2][:, mc, :], PB[mc][:, :],
                        XNTs[tt2 % 2][:, mc, :], ALU.add),
                        waits=[wopetok, t_x2sp] if mc == 0 else (), sem=DV)
                t_x2sp = p.dma("sync", x2_d[:, :, tsl2].rearrange("fc q t -> q fc t"),
                               X2Ss[tt2 % 2][:, :, :], waits=[t_x2], sem=SP, stream="x2sp")
                t_xn_h[tt2 % 2] = t_x2

            for tt in range(NTT):
                tsl = slice(tt * TTW, (tt + 1) * TTW)
                lw = ([t_acar] + gB) if tt == 0 else (t_ld_h[tt % 2] or [])
                t_lda = p.dma("sync", LAs[tt % 2][:, :, :],
                              A_d[:, :, tsl].rearrange("fc q t -> q fc t"),
                              waits=lw, sem=SP, stream="ldA")
                t_ldb = p.dma("sync", LBs[tt % 2][:, :, :],
                              B_d[:, :, tsl].rearrange("fc q t -> q fc t"),
                              waits=[t_rw_tt[-1]] if t_rw_tt else [], sem=SP,
                              stream="ldB")
                t_ldr = p.dma("sync", LRs[tt % 2][:, :, :],
                              RS_d[:, :, tsl].rearrange("fc q t -> q fc t"),
                              waits=[t_rw_tt[-1]] if t_rw_tt else [], sem=SP,
                              stream="ldR")
                t_ldab = p.dma("sync", XNTs[tt % 2][:, :, :],
                               xnfm_d[:, :, tsl].rearrange("fc q t -> q fc t"),
                               waits=[t_xn_h[tt % 2]] if t_xn_h[tt % 2] is not None else [],
                               sem=SP, stream="ldXN")
                LA, LB, LR, XNT = LAs[tt % 2], LBs[tt % 2], LRs[tt % 2], XNTs[tt % 2]
                RWB = RWBs[tt % 2]
                t_rw = None
                for fc in range(FC):
                    dv(lambda e, fc=fc, tsl=tsl: e.tensor_scalar(
                        EWTT, ramp[:, tsl], dcols[:, fc, 4:5], None, ALU.mult),
                       waits=[t_ldab, t_lda, t_ldb, t_ldr, t_acar] if fc == 0 else ())
                    ac(lambda e: e.activation(EWTT, EWTT, AF.Exp),
                       waits=[(p.auto["vector"], p.auto["vector"].n)])
                    dv(lambda e, fc=fc, LA=LA: e.scalar_tensor_tensor(
                        LA[:, fc, :], EWTT, CARR[:, fc, 2:3], LA[:, fc, :],
                        ALU.mult, ALU.add), waits=[(p.auto["scalar"], p.auto["scalar"].n)])
                    dv(lambda e, fc=fc, LB=LB: e.scalar_tensor_tensor(
                        LB[:, fc, :], EWTT, CARR[:, fc, 3:4], LB[:, fc, :],
                        ALU.mult, ALU.add))
                    t_rec = dv(lambda e, fc=fc, LB=LB: e.reciprocal(
                        EWT2[fc % 2][:, :], LB[:, fc, :]),
                        waits=[t_pl_h[fc % 2]] if t_pl_h[fc % 2] is not None else (),
                        sem=DV)
                    gp(lambda e, fc=fc, LA=LA: e.tensor_tensor(
                        LA[:, fc, :], LA[:, fc, :], EWT2[fc % 2][:, :], ALU.mult),
                       waits=[t_rec])
                    t_rw = gp(lambda e, fc=fc, LA=LA, LR=LR, RWB=RWB: e.tensor_tensor(
                        RWB[:, fc, :], LR[:, fc, :], LA[:, fc, :], ALU.mult),
                        waits=[t_wope] if (fc == 0 and t_wope is not None) else (), sem=PL)
                    t_pl_h[fc % 2] = t_rw
                t_rw_tt.append(t_rw)
                if stage == "wkv":
                    t_wsp = p.dma("sync", RS_d[:, :, tsl].rearrange("fc q t -> q fc t"),
                                  LA[:, :, :], waits=[t_rw], sem=SP, stream="wkvsp")
                else:
                    t_wsp = t_rw
                if tt > 0:
                    emit_x2evac(tt - 1)
                for mc in range(FC):
                    for kc in range(FC):
                        t_wope = te(lambda e, mc=mc, kc=kc, RWB=RWB: e.matmul(
                            PB[mc][:, :], WOb[:, kc, mc * 128:(mc + 1) * 128], RWB[:, kc, :],
                            start=(kc == 0), stop=(kc == FC - 1)),
                            waits=[t_rw, t_wo, t_x2] if (mc == 0 and kc == 0) else (),
                            sem=PE if kc == FC - 1 else None)
                _x2info[tt] = (t_wope, tsl)
                t_ld_h[tt % 2] = [t_wsp]
            emit_x2evac(NTT - 1)
            t_ev2_h = [None, None]
            t_bn2 = None
            t_tr2 = None
            t_tr2_h = [None, None]
            for i in range(NT):
                pb0 = (i % 2) * 2
                t_ldx2b = p.dma("sync", X2Ft[i % 2][:, :, :],
                                x2_d[:, :, i * 128:(i + 1) * 128].rearrange("fc q t -> q fc t"),
                                waits=([t_x2sp] if i < 2 else [t_tr2_h[i % 2]]),
                                sem=SP, stream="x2ld")
                for fc in range(FC):
                    t_tr2 = te(lambda e, fc=fc, i=i, pb0=pb0: e.transpose(
                        PB[pb0 + fc // 4][:, (fc % 4) * 128:(fc % 4) * 128 + 128],
                        X2Ft[i % 2][:, fc, :], id32[:, :]),
                        waits=([t_ldx2b] + ([t_ev2_h[i % 2]] if t_ev2_h[i % 2] is not None
                                            else [])) if fc == 0 else (),
                        sem=PE if fc == FC - 1 else None)
                t_tr2_h[i % 2] = t_tr2
                ac(lambda e, i=i, pb0=pb0: e.copy(X2TMb[i % 2][:, 0:512], PB[pb0][:, :]),
                   waits=[t_tr2] + ([t_bn2] if i > 0 else []))
                t_ev2 = ac(lambda e, i=i, pb0=pb0: e.copy(X2TMb[i % 2][:, 512:1024],
                                                          PB[pb0 + 1][:, :]), sem=ACT_S)
                t_ev2_h[i % 2] = t_ev2
                dv(lambda e, i=i: e.bn_stats(ST6b[:, 0:6], X2TMb[i % 2][:, 0:512]),
                   waits=[t_ev2])
                dv(lambda e, i=i: e.bn_stats(ST6b[:, 6:12], X2TMb[i % 2][:, 512:1024]))
                dv(lambda e, i=i: e.bn_aggr(LN2H[:, i, 0:2], ST6b[:, 0:6]))
                t_bn2 = dv(lambda e, i=i: e.bn_aggr(LN2H[:, i, 2:4], ST6b[:, 6:12]), sem=DV)
                t_bn2w[i % 2] = t_bn2
            t_cmb = combine2(LN2H, LN2S[:, :, 0], LN2S[:, :, 1], t_bn2)
            t_rstd2 = rstd_batch(LN2S[:, :, 2], LN2S[:, :, 1], 1.0, LN_EPS, t_cmb)
            t_z2 = None
            t_yev = None
            t_yev_h = [None, None]
            t_zev_h = [None, None]
            t_zf_h = [None, None]
            for i in range(NT):
                pbf = 4 + (i % 2) * 2
                pbb = (i % 2) * 2
                t_ldx2c = p.dma("sync", X2Ft[i % 2][:, :, :],
                                x2_d[:, :, i * 128:(i + 1) * 128].rearrange("fc q t -> q fc t"),
                                waits=([t_tr2_h[i % 2]] if i < 2 else [t_zf_h[i % 2]]),
                                sem=SP, stream="x2ld2")
                for fc in range(FC):
                    t_trf2 = te(lambda e, fc=fc, i=i, pbf=pbf: e.transpose(
                        PB[pbf + fc // 4][:, (fc % 4) * 128:(fc % 4) * 128 + 128],
                        X2Ft[i % 2][:, fc, :], id32[:, :]),
                        waits=([t_ldx2c, t_rstd2, t_ev2, t_zev_h[i % 2]]
                               if fc == 0 else ()),
                        sem=PE if fc == FC - 1 else None)
                t_zf_h[i % 2] = t_trf2
                t_zev = ac(lambda e, i=i, pbf=pbf: e.copy(
                    Z2T[:, 0:512], PB[pbf][:, :]), waits=[t_trf2, t_z2])
                t_zev = ac(lambda e, i=i, pbf=pbf: e.copy(
                    Z2T[:, 512:1024], PB[pbf + 1][:, :]), sem=ACT_S)
                t_zev_h[i % 2] = t_zev
                t_z2 = dv(lambda e, i=i: e.tensor_scalar(
                    Z2T[:, :], Z2T[:, :],
                    LN2S[:, i, 0:1], LN2S[:, i, 2:3], ALU.subtract, ALU.mult),
                    waits=[t_zev], sem=DV)
                for fc in range(FC):
                    t_trb = te(lambda e, fc=fc, i=i, pbb=pbb: e.transpose(
                        PB[pbb + fc // 4][:, (fc % 4) * 128:(fc % 4) * 128 + 128],
                        Z2T[:, fc * 128:(fc + 1) * 128], id32[:, :]),
                        waits=([t_z2] + ([t_yev_h[i % 2]] if t_yev_h[i % 2] is not None
                                         else [])) if fc == 0 else (),
                        sem=PE if fc == FC - 1 else None)
                for fc in range(FC):
                    t_yev = ac(lambda e, fc=fc, i=i, pbb=pbb: e.activation(
                        XN1[:, fc, 8 + i * 128:8 + (i + 1) * 128],
                        PB[pbb + fc // 4][:, (fc % 4) * 128:(fc % 4) * 128 + 128],
                        AF.Identity, scale=vecs[:, fc, 2:3], bias=vecs[:, fc, 3:4]),
                        waits=[t_trb] if fc == 0 else (), sem=ACT_S)
                t_yev_h[i % 2] = t_yev
            t_yl = None
            for fc in range(FC):
                t_yl = dv(lambda e, fc=fc: e.tensor_copy(PRV3[:, fc:fc + 1],
                                                         XN1[:, fc, 7 + T:8 + T]),
                          waits=[t_yev] if fc == 0 else (), sem=DV)
            for fc in range(FC):
                j, off = divmod(fc * 128, 512)
                t_r2t = te(lambda e, fc=fc, j=j, off=off: e.transpose(
                    PB[j][0:1, off:off + 128], PRV3[:, fc:fc + 1], id32[:, :]),
                    waits=[t_yl] if fc == 0 else (), sem=PE if fc == FC - 1 else None)
            for j in range(2):
                t_evr2 = dv(lambda e, j=j: e.tensor_copy(SR2[:, j * 512:(j + 1) * 512],
                                                         PB[j][0:1, :]),
                            waits=[t_r2t] if j == 0 else (), sem=DV)
            t_pub3 = p.dma("sync", r2_in[:, :], SR2[:, :], waits=[t_evr2], sem=SP, stream="d31")
            t_r2 = p.op("gpsimd", lambda e: e.collective_compute(
                "AllGather", ALU.bypass, replica_groups=[list(range(NCORE))],
                ins=[r2_in[:, :]], outs=[r2_out[:, :]]), waits=[t_pub3], sem=CC, chain=False)
            for fc in range(FC):
                t_prv3 = p.op("gpsimd", lambda e, fc=fc: e.indirect_dma_start(
                    out=PRV3[:, fc:fc + 1], out_offset=None,
                    in_=r2_out.rearrange("c (d o) -> (c d) o", o=1),
                    in_offset=IndirectOffsetOnAxis(ap=pidx[:, fc:fc + 1], axis=0)),
                    waits=[t_r2] if fc == 0 else (), sem=p.ssem("ind4"), amt=16, chain=False)
            t_ynp = None
            for fc in range(FC):
                t_ynp = dv(lambda e, fc=fc: e.tensor_scalar(
                    XN1[:, fc, 7:8], PRV3[:, fc:fc + 1], f1c, None, ALU.mult),
                    waits=[t_prv3] if fc == 0 else (), sem=DV)

            if stage == "yn":
                tend = p.dma("gpsimd", out_h.rearrange("(q r) d -> q (r d)", q=128),
                             XN1[:, :, 8:8 + T], waits=[t_yev, t_ynp2] if False else [t_yev],
                             sem=ST, stream="yndump")
                p.wait("gpsimd", tend)
                pB.close()
                pmid.close()
                block = ctx.enter_context(nc.Block())
                p.realize(block)
                return nc
            g2 = gate()
            pB.close()
            p2 = ExitStack()
            sb2 = lambda n, sh, d=F32: p2.enter_context(nc.sbuf_tensor(n, sh, d))
            YMIX = sb2("YMIX", [128, FC, T], BF16)
            SFF = sb2("SFF", [128, FC, T])
            KRT = sb2("KRT", [128, 512])
            p2w = ExitStack()
            KKm = p2w.enter_context(nc.sbuf_tensor("KKm", [128, 4, T], BF16))
            WFKs = [p2w.enter_context(nc.sbuf_tensor(f"WFKs{j}", [128, FC, 512], BF16))
                    for j in range(2)]
            WFVs = [p2w.enter_context(nc.sbuf_tensor(f"WFVs{j}", [128, 4, D], BF16))
                    for j in range(2)]
            PC = [p2.enter_context(nc.psum_tensor(f"PC{j}", [128, 512], F32))
                  for j in range(8)]

            t_ym = None
            for fc in range(FC):
                dv(lambda e, fc=fc: e.tensor_tensor(
                    SFF[:, fc, :], XN1[:, fc, 8:8 + T], XN1[:, fc, 7:7 + T], ALU.add),
                   waits=(g2 + [t_ynp]) if fc == 0 else ())
                t_ym = dv(lambda e, fc=fc: e.tensor_scalar(
                    YMIX[:, fc, :], SFF[:, fc, :], 0.5, None, ALU.mult), sem=DV)

            t_kkev = None
            t_kk_mcc = [None, None, None, None]
            t_sfe = None
            t_sfe_h = [None, None, None, None]
            t_dn2 = None
            t_wk_hist = [None, None]
            for mb in range(8):
                wfk, wfv = WFKs[mb % 2], WFVs[mb % 2]
                t_fr = t_wk_hist[mb % 2]
                t_wfkld = p.dma("gpsimd", wfk[:, :, :],
                                wfk_h[:, mb * 512:(mb + 1) * 512].rearrange(
                                    "(kc q) o -> q kc o", q=128),
                                waits=[t_fr] if t_fr is not None else g2, sem=LD,
                                stream="wfk")
                t_wfvld = p.dma("gpsimd", wfv[:, :, :],
                                wfv_h[mb * 512:(mb + 1) * 512, :].rearrange(
                                    "(g q) o -> q g o", q=128),
                                waits=[t_fr] if t_fr is not None else g2,
                                sem=LD, stream="wfv")
                for nt in range(4):
                    for mcc in range(4):
                        for kc in range(FC):
                            t_kkpe = te(lambda e, mcc=mcc, kc=kc, nt=nt, wfk=wfk: e.matmul(
                                PC[mcc][:, :], wfk[:, kc, mcc * 128:(mcc + 1) * 128],
                                YMIX[:, kc, nt * 512:(nt + 1) * 512],
                                start=(kc == 0), stop=(kc == FC - 1)),
                                waits=([t_ym, t_wfkld, t_wfvld, t_kkev, t_sfe, t_dn2]
                                       if (nt == 0 and mcc == 0) else
                                       ([t_kk_mcc[mcc]] if t_kk_mcc[mcc] is not None else []))
                                if kc == 0 else (),
                                sem=PE if kc == FC - 1 else None)
                    for mcc in range(4):
                        ac(lambda e, mcc=mcc: e.activation(
                            KRT[:, :], PC[mcc][:, :], AF.Relu),
                           waits=([t_kkpe] if mcc == 0 else []) +
                                 [(p.auto["vector"], p.auto["vector"].n)])
                        t_kkev = dv(lambda e, mcc=mcc, nt=nt: e.tensor_tensor(
                            KKm[:, mcc, nt * 512:(nt + 1) * 512], KRT[:, :], KRT[:, :],
                            ALU.mult),
                            waits=[(p.auto["scalar"], p.auto["scalar"].n)], sem=DV)
                        t_kk_mcc[mcc] = t_kkev
                for nt in range(4):
                    for oc in range(FC):
                        for sub in range(4):
                            t_dn2 = te(lambda e, oc=oc, sub=sub, nt=nt, wfv=wfv: e.matmul(
                                PC[4 + oc % 4][:, :], wfv[:, sub, oc * 128:(oc + 1) * 128],
                                KKm[:, sub, nt * 512:(nt + 1) * 512],
                                start=(sub == 0), stop=(sub == 3)),
                                waits=([t_kkev, t_sfe] if (nt == 0 and oc == 0) else
                                       ([t_sfe_h[oc % 4]] if t_sfe_h[oc % 4] is not None else []))
                                if sub == 0 else (),
                                sem=PE if sub == 3 else None)
                        if mb == 0:
                            t_sfe = dv(lambda e, oc=oc, nt=nt: e.tensor_copy(
                                SFF[:, oc, nt * 512:(nt + 1) * 512], PC[4 + oc % 4][:, :]),
                                waits=[t_dn2], sem=DV)
                            t_sfe_h[oc % 4] = t_sfe
                        else:
                            t_sfe = dv(lambda e, oc=oc, nt=nt: e.tensor_tensor(
                                SFF[:, oc, nt * 512:(nt + 1) * 512],
                                SFF[:, oc, nt * 512:(nt + 1) * 512],
                                PC[4 + oc % 4][:, :], ALU.add),
                                waits=[t_dn2], sem=DV)
                        t_sfe_h[oc % 4] = t_sfe
                t_wk_hist[mb % 2] = t_dn2

            g2b = gate()
            p2w.close()
            WFRb = p2.enter_context(nc.sbuf_tensor("WFRb", [128, FC, D], BF16))
            RRTt = p2.enter_context(nc.sbuf_tensor("RRTt", [128, FC, 512], F32))
            X2Tb = p2.enter_context(nc.sbuf_tensor("X2Tb", [128, FC, 512], F32))
            SST = p2.enter_context(nc.sbuf_tensor("SST", [128, 512], F32))
            STMb = p2.enter_context(nc.sbuf_tensor("STMb", [128, D], F32))
            t_wfr = p.dma("gpsimd", WFRb[:, :, :],
                          wfr_h.rearrange("(kc q) o -> q kc o", q=128),
                          waits=g2b, sem=LD, stream="wfr")
            t_sev = None
            t_stm = None
            t_ssp = None
            for tt in range(NTT):
                tsl = slice(tt * 512, (tt + 1) * 512)
                cur = slice(8 + tt * 512, 8 + (tt + 1) * 512)
                t_ldx2 = p.dma("sync", X2Tb[:, :, :],
                               x2_d[:, :, tsl].rearrange("fc q t -> q fc t"),
                               waits=(g2b if tt == 0 else [t_stm]), sem=SP, stream="x2r")
                for mc in range(FC):
                    for kc in range(FC):
                        t_rrpe = te(lambda e, mc=mc, kc=kc, cur=cur: e.matmul(
                            PC[mc][:, :], WFRb[:, kc, mc * 128:(mc + 1) * 128],
                            XN1[:, kc, cur], start=(kc == 0), stop=(kc == FC - 1)),
                            waits=[t_wfr, t_sfe, t_kkev, t_sev, t_stm,
                                   (p.auto["scalar"], p.auto["scalar"].n)]
                            if (mc == 0 and kc == 0) else (),
                            sem=PE if kc == FC - 1 else None)
                t_rrev = None
                for mc in range(FC):
                    t_rrev = ac(lambda e, mc=mc: e.activation(
                        RRTt[:, mc, :], PC[mc][:, :], AF.Sigmoid),
                        waits=[t_rrpe] if mc == 0 else (), sem=ACT_S)
                for mc in range(FC):
                    dv(lambda e, mc=mc, tsl=tsl: e.tensor_tensor(
                        SST[:, :], RRTt[:, mc, :], SFF[:, mc, tsl], ALU.mult),
                       waits=[t_rrev, t_ldx2] if mc == 0 else ())
                    t_sev = dv(lambda e, mc=mc: e.tensor_tensor(
                        X2Tb[:, mc, :], SST[:, :], X2Tb[:, mc, :], ALU.add), sem=DV)
                for tg in range(4):
                    for fc in range(FC):
                        t_stp = te(lambda e, fc=fc, tg=tg: e.transpose(
                            PC[(tg % 2) * 2 + (fc // 4)][:, (fc % 4) * 128:(fc % 4) * 128 + 128],
                            X2Tb[:, fc, tg * 128:(tg + 1) * 128], id32[:, :]),
                            waits=[t_sev, t_stm] if fc == 0 else (),
                            sem=PE if fc == FC - 1 else None)
                    b0, b1 = (tg % 2) * 2, (tg % 2) * 2 + 1
                    ac(lambda e, b0=b0: e.copy(STMb[:, 0:512], PC[b0][:, :]),
                       waits=[t_stp, t_ssp])
                    t_stm = ac(lambda e, b1=b1: e.copy(STMb[:, 512:1024], PC[b1][:, :]),
                               sem=ACT_S)
                    t_ssp = p.dma("sync", stm_d[(tt * 4 + tg) * 128:(tt * 4 + tg + 1) * 128, :],
                                  STMb[:, :], waits=[t_stm], sem=SP, stream="sspill")
            if stage in ("s", "wkv", "x2"):
                src = {"s": stm_d, "wkv": RS_d, "x2": x2_d}[stage]
                if stage == "s":
                    tend = p.dma("sync", out_h.rearrange("(o a) b -> o (a b)", o=1),
                                 src.rearrange("(o a) b -> o (a b)", o=1),
                                 waits=gate(), sem=ST, stream="d37")
                else:
                    tend = p.dma("sync", out_h.rearrange("(o a) b -> o (a b)", o=1),
                                 src.rearrange("(o f) q t -> o (f q t)", o=1),
                                 waits=gate(), sem=ST, stream="d38")
                p.wait("sync", tend)
                p2.close()
                pmid.close()
                block = ctx.enter_context(nc.Block())
                p.realize(block)
                return nc
            g3 = gate()
            p2.close()
            pmid.close()

            p3 = ExitStack()
            sb3 = lambda n, sh, d=F32: p3.enter_context(nc.sbuf_tensor(n, sh, d))
            EOTs = [ctx.enter_context(nc.sbuf_tensor(f"EOT{j}", [128, D], F32))
                    for j in range(2)]
            XID = sb3("XID", [128, CAP // 128], U32)
            XGF = sb3("XGF", [128, FC, CAP], BF16)
            EACC = sb3("EACC", [128, FC, CAP])
            HEs = sb3("HEs", [128, 4, 512], BF16)
            GS1 = sb3("GS1", [128, 512])
            RGB = [sb3(f"RGB{j}", [128, FC, 512], BF16) for j in range(2)]
            RUB = [sb3(f"RUB{j}", [128, FC, 512], BF16) for j in range(2)]
            RDB = [sb3(f"RDB{j}", [128, 4, D], BF16) for j in range(2)]
            p3t = ExitStack()
            TBL8 = p3t.enter_context(nc.sbuf_tensor("TBL8", [8, E * CAP // 8], F32))
            MYROW = p3t.enter_context(nc.sbuf_tensor("MYROW", [1, CAP], F32))
            IDXT = p3t.enter_context(nc.sbuf_tensor("IDXT", [128, CAP // 128], F32))

            t_tbl = p.dma("sync", TBL8[:, :], tbl_out.rearrange("(e q) o -> e (q o)", e=8),
                          waits=g3 + [t_ar], sem=LD, stream="d39")
            p3a = ExitStack()
            PDS = p3a.enter_context(nc.psum_tensor("PDS", [128, 512], F32))
            PX0 = p3a.enter_context(nc.psum_tensor("PX0", [128, 1024], BF16))
            PX1 = p3a.enter_context(nc.psum_tensor("PX1", [128, 1024], BF16))
            t_myr = None
            t_myr2 = None
            for j in range(CAP // 512):
                t_myr = te(lambda e, j=j: e.matmul(
                    PDS[0:1, :], aux8[:, 1:2], TBL8[:, j * 512:(j + 1) * 512],
                    start=True, stop=True),
                    waits=(g3 + [t_tbl]) if j == 0 else [t_myr2], sem=PE)
                t_myr2 = dv(lambda e, j=j: e.tensor_copy(MYROW[:, j * 512:(j + 1) * 512],
                                                         PDS[0:1, :]),
                            waits=[t_myr], sem=DV)
            t_myd = p.dma("sync", myrow_d.rearrange("(o q) z -> o (q z)", o=1), MYROW[:, :],
                          waits=[t_myr2], sem=SP, stream="d40")
            t_myb = p.dma("sync", IDXT[:, :], myrow_d.rearrange("(q j) o -> q (j o)", q=128),
                          waits=[t_myd], sem=SP, stream="d41")
            t_xid = dv(lambda e: e.tensor_copy(XID[:, :], IDXT[:, :]), waits=[t_myb], sem=DV)
            p3t.close()

            with nc.sbuf_tensor("XGT", [128, 4, D], BF16) as XGT:
                t_xge = None
                t_xtr = None
                t_xg = None
                for batch in range(CAP // 512):
                    for jj in range(4):
                        j = batch * 4 + jj
                        t_xg = p.op("gpsimd", lambda e, j=j, jj=jj: e.indirect_dma_start(
                            out=XGT[:, jj, :], out_offset=None,
                            in_=ag1_out[:, :],
                            in_offset=IndirectOffsetOnAxis(ap=XID[:, j:j + 1], axis=0)),
                            waits=([t_xid, t_ag1] if j == 0 else []) +
                                  ([t_xge] if (jj == 0 and t_xge is not None) else []),
                            sem=p.ssem("ind5"), amt=16, chain=False)
                    for jj in range(4):
                        j = batch * 4 + jj
                        px = PX0 if jj % 2 == 0 else PX1
                        for fc in range(FC):
                            t_xtr = te(lambda e, jj=jj, fc=fc, px=px: e.transpose(
                                px[:, fc * 128:(fc + 1) * 128],
                                XGT[:, jj, fc * 128:(fc + 1) * 128], idbf[:, :]),
                                waits=([t_xg, t_myr] if jj == 0 else
                                       ([t_xge] if jj >= 2 else [])) if fc == 0 else (),
                                sem=PE if fc == FC - 1 else None)
                        t_xge = dv(lambda e, j=j, px=px: e.tensor_copy(
                            XGF[:, :, j * 128:(j + 1) * 128].rearrange("q fc t -> q fc t"),
                            px.ap().rearrange("q (fc t) -> q fc t", fc=FC)),
                            waits=[t_xtr], sem=DV)
            p3a.close()
            p3ps = ExitStack()
            PD = [p3ps.enter_context(nc.psum_tensor(f"PD{j}", [128, 512], F32))
                  for j in range(8)]

            NST = CAP // 512
            t_hev = None
            t_hev_h = [None, None]
            t_dse = None
            t_dse2 = None
            t_eev = None
            t_dpe = None
            t_rgl = t_rul = t_rdl = None
            t_dpe_hist = [None, None]
            for mcb in range(8):
                rgb, rub, rdb = RGB[mcb % 2], RUB[mcb % 2], RDB[mcb % 2]
                t_free = t_dpe_hist[mcb % 2]
                t_rgl = p.dma("gpsimd", rgb[:, :, :],
                              rg_h[:, mcb * 512:(mcb + 1) * 512].rearrange(
                                  "(kc q) o -> q kc o", q=128),
                              waits=([t_free] if t_free is not None else g3), sem=LD,
                              stream="rg")
                t_rul = p.dma("gpsimd", rub[:, :, :],
                              ru_h[:, mcb * 512:(mcb + 1) * 512].rearrange(
                                  "(kc q) o -> q kc o", q=128), sem=LD, stream="ru")
                t_rdl = p.dma("gpsimd", rdb[:, :, :],
                              rd_h[mcb * 512:(mcb + 1) * 512, :].rearrange(
                                  "(g q) o -> q g o", q=128), sem=LD, stream="rd")
                for n in range(NST):
                    ssl = slice(n * 512, (n + 1) * 512)
                    for mc in range(4):
                        j0, j1 = (mc % 2) * 2, (mc % 2) * 2 + 1
                        for kc in range(FC):
                            te(lambda e, mc=mc, kc=kc, j0=j0, rgb=rgb, ssl=ssl: e.matmul(
                                PD[j0][:, :], rgb[:, kc, mc * 128:(mc + 1) * 128],
                                XGF[:, kc, ssl], start=(kc == 0), stop=(kc == FC - 1)),
                                waits=[t_rgl, t_rul, t_rdl, t_xge, t_hev, t_dse, t_eev]
                                if (n == 0 and mc == 0 and kc == 0) else
                                (([t_hev_h[mc % 2]] if t_hev_h[mc % 2] is not None else [])
                                 if kc == 0 else ()))
                        for kc in range(FC):
                            t_upe = te(lambda e, mc=mc, kc=kc, j1=j1, rub=rub, ssl=ssl: e.matmul(
                                PD[j1][:, :], rub[:, kc, mc * 128:(mc + 1) * 128],
                                XGF[:, kc, ssl], start=(kc == 0), stop=(kc == FC - 1)),
                                sem=PE if kc == FC - 1 else None)
                        ac(lambda e, j0=j0: e.activation(GS1[:, :], PD[j0][:, :], AF.Silu),
                           waits=[t_upe, (p.auto["vector"], p.auto["vector"].n)])
                        t_hev = dv(lambda e, mc=mc, j1=j1: e.tensor_tensor(
                            HEs[:, mc, :], GS1[:, :], PD[j1][:, :], ALU.mult),
                            waits=[(p.auto["scalar"], p.auto["scalar"].n)], sem=DV)
                        t_hev_h[mc % 2] = t_hev
                    for oc in range(FC):
                        for sub in range(4):
                            t_dpe = te(lambda e, oc=oc, sub=sub, rdb=rdb: e.matmul(
                                PD[4 + oc % 4][:, :] if False else PD[oc % 4 + 4][:, :],
                                rdb[:, sub, oc * 128:(oc + 1) * 128],
                                HEs[:, sub, :], start=(sub == 0), stop=(sub == 3)),
                                waits=([t_hev, t_dse2] if oc == 0 else [t_dse2])
                                if sub == 0 else (),
                                sem=PE if sub == 3 else None)
                        if mcb == 0:
                            t_dse2 = dv(lambda e, oc=oc, n=n: e.tensor_copy(
                                EACC[:, oc, n * 512:(n + 1) * 512], PD[oc % 4 + 4][:, :]),
                                waits=[t_dpe], sem=DV)
                        else:
                            t_dse2 = dv(lambda e, oc=oc, n=n: e.tensor_tensor(
                                EACC[:, oc, n * 512:(n + 1) * 512],
                                EACC[:, oc, n * 512:(n + 1) * 512],
                                PD[oc % 4 + 4][:, :], ALU.add),
                                waits=[t_dpe], sem=DV)
                    t_dse = t_dse2
                t_dpe_hist[mcb % 2] = t_dpe
            t_etr = None
            t_eev = None
            t_esp = None
            t_eev_h = [None, None]
            t_esp_h = [None, None]
            for j in range(CAP // 128):
                pb0 = (j % 2) * 2
                for fc in range(FC):
                    t_etr = te(lambda e, fc=fc, j=j, pb0=pb0: e.transpose(
                        PD[pb0 + fc // 4][:, (fc % 4) * 128:(fc % 4) * 128 + 128],
                        EACC[:, fc, j * 128:(j + 1) * 128], id32[:, :]),
                        waits=([t_dse, (p.auto["scalar"], p.auto["scalar"].n)]
                               if j < 2 else [t_eev_h[j % 2]]) if fc == 0 else (),
                        sem=PE if fc == FC - 1 else None)
                ac(lambda e, pb0=pb0, j=j: e.copy(EOTs[j % 2][:, 0:512], PD[pb0][:, :]),
                   waits=[t_etr] + ([t_esp_h[j % 2]] if t_esp_h[j % 2] is not None else []))
                t_eev = ac(lambda e, pb0=pb0, j=j: e.copy(EOTs[j % 2][:, 512:1024],
                                                          PD[pb0 + 1][:, :]), sem=ACT_S)
                t_eev_h[j % 2] = t_eev
                t_esp = p.dma("sync", ag4_in.rearrange("(q j) o -> q j o", j=CAP // 128)[:, j:j + 1, :],
                              EOTs[j % 2].ap().rearrange("q (j o) -> q j o", j=1),
                              waits=[t_eev], sem=SP, stream="eout")
                t_esp_h[j % 2] = t_esp
            t_ag4 = p.op("gpsimd", lambda e: e.collective_compute(
                "AllGather", ALU.bypass, replica_groups=[list(range(NCORE))],
                ins=[ag4_in[:, :]], outs=[ag4_out[:, :]]), waits=[t_esp], sem=CC, chain=False)

            g4 = gate()
            p3ps.close()
            p3.close()
            p4 = ExitStack()
            SACC = p4.enter_context(nc.sbuf_tensor("SACC", [128, FC, T], F32))
            p4w = ExitStack()
            sb4 = lambda n, sh, d=F32: p4w.enter_context(nc.sbuf_tensor(n, sh, d))
            XBF = sb4("XBF", [128, FC, T], BF16)
            HB = sb4("HB", [128, 8, T], BF16)
            GS2 = sb4("GS2", [128, 512])
            SGB = [sb4(f"SGB{j}", [128, FC, 512], BF16) for j in range(2)]
            SUB = [sb4(f"SUB{j}", [128, FC, 512], BF16) for j in range(2)]
            SDB = [sb4(f"SDB{j}", [128, 4, D], BF16) for j in range(2)]
            PS = [p4.enter_context(nc.psum_tensor(f"PS{j}", [128, 512], F32))
                  for j in range(8)]
            t_xbf = p.dma("sync", XBF[:, :, :], xnbf_d[:, :, :].rearrange("fc q t -> q fc t"),
                          waits=[t_dpe_hist[0], t_dpe_hist[1], t_dse, t_etr],
                          sem=LD, stream="d46")
            NMB = DSH // 1024
            t_eevP3 = t_eev
            t_sde_h = [None, None]
            t_h2_nt = [None, None, None, None]
            t_h2 = None
            t_sde = None
            t_sgl = t_sul = t_sdl = None
            t_d2 = None
            for mb in range(NMB):
                for half in range(2):
                    mcb = mb * 2 + half
                    sgb, sub_ = SGB[mcb % 2], SUB[mcb % 2]
                    t_sgl = p.dma("gpsimd", sgb[:, :, :],
                                  sg_h[:, mcb * 512:(mcb + 1) * 512].rearrange(
                                      "(kc q) o -> q kc o", q=128),
                                  waits=[t_h2] if t_h2 is not None else
                                        [t_etr, t_eev_h[0], t_eev_h[1]],
                                  sem=LD, stream="d47")
                    t_sul = p.dma("gpsimd", sub_[:, :, :],
                                  su_h[:, mcb * 512:(mcb + 1) * 512].rearrange(
                                      "(kc q) o -> q kc o", q=128), sem=LD, stream="d48")
                    for mc in range(4):
                        for nt in range(4):
                            j0, j1 = nt, 4 + nt
                            for kc in range(FC):
                                gw = []
                                if kc == 0:
                                    if mc == 0 and nt == 0:
                                        gw += [t_sgl, t_sul, t_xbf, t_sde_h[0],
                                               t_sde_h[1], t_eevP3]
                                    if t_h2_nt[nt] is not None:
                                        gw.append(t_h2_nt[nt])
                                te(lambda e, mc=mc, kc=kc, nt=nt, j0=j0, sgb=sgb: e.matmul(
                                    PS[j0][:, :], sgb[:, kc, mc * 128:(mc + 1) * 128],
                                    XBF[:, kc, nt * 512:(nt + 1) * 512],
                                    start=(kc == 0), stop=(kc == FC - 1)),
                                    waits=gw)
                            for kc in range(FC):
                                t_u2 = te(lambda e, mc=mc, kc=kc, nt=nt, j1=j1, sub_=sub_: e.matmul(
                                    PS[j1][:, :], sub_[:, kc, mc * 128:(mc + 1) * 128],
                                    XBF[:, kc, nt * 512:(nt + 1) * 512],
                                    start=(kc == 0), stop=(kc == FC - 1)),
                                    waits=[t_sul] if (mc == 0 and nt == 0 and kc == 0) else (),
                                    sem=PE if kc == FC - 1 else None)
                            ac(lambda e, j0=j0: e.activation(GS2[:, :], PS[j0][:, :], AF.Silu),
                               waits=[t_u2, (p.auto["vector"], p.auto["vector"].n)])
                            t_h2 = dv(lambda e, half=half, mc=mc, nt=nt, j1=j1: e.tensor_tensor(
                                HB[:, half * 4 + mc, nt * 512:(nt + 1) * 512],
                                GS2[:, :], PS[j1][:, :], ALU.mult),
                                waits=[(p.auto["scalar"], p.auto["scalar"].n)], sem=DV)
                            t_h2_nt[nt] = t_h2
                if stage == "hb" and mb == int(os.environ.get('HBMB', '0')):
                    tend = p.dma("gpsimd", out_h.rearrange("(o a) b -> o (a b)", o=1).rearrange(
                        "o (q r) -> q (o r)", q=128), HB[:, :, :], waits=[t_h2], sem=ST, stream="d49")
                    p.wait("gpsimd", tend)
                    p4w.close()
                    p4.close()
                    block = ctx.enter_context(nc.Block())
                    p.realize(block)
                    return nc
                pass
                t_sdl0 = p.dma("gpsimd", SDB[0][:, :, :],
                               sd_h[mb * 1024:mb * 1024 + 512, :].rearrange(
                                   "(g q) o -> q g o", q=128),
                               waits=[t_d2] if t_d2 is not None else [], sem=LD, stream="d50")
                t_sdl = p.dma("gpsimd", SDB[1][:, :, :],
                              sd_h[mb * 1024 + 512:mb * 1024 + 1024, :].rearrange(
                                  "(g q) o -> q g o", q=128), sem=LD, stream="d51")
                t_sde_h = [t_sde, t_sde]
                for oc in range(8):
                    bs = (oc % 2) * 4
                    for kc2 in range(8):
                        for nt in range(4):
                            t_d2 = te(lambda e, kc2=kc2, oc=oc, bs=bs, nt=nt: e.matmul(
                                PS[bs + nt][:, :],
                                SDB[kc2 // 4][:, kc2 % 4, oc * 128:(oc + 1) * 128],
                                HB[:, kc2, nt * 512:(nt + 1) * 512],
                                start=(kc2 == 0), stop=(kc2 == 7)),
                                waits=([t_h2, t_sdl, t_sde_h[oc % 2]]
                                       if oc < 2 else [t_sde_h[oc % 2]])
                                if (kc2 == 0 and nt == 0) else (),
                                sem=PE if (kc2 == 7 and nt == 3) else None)
                    for nt in range(4):
                        if mb == 0:
                            t_sde = dv(lambda e, oc=oc, bs=bs, nt=nt: e.tensor_copy(
                                SACC[:, oc, nt * 512:(nt + 1) * 512], PS[bs + nt][:, :]),
                                waits=[t_d2] if nt == 0 else (), sem=DV)
                        else:
                            t_sde = dv(lambda e, oc=oc, bs=bs, nt=nt: e.tensor_tensor(
                                SACC[:, oc, nt * 512:(nt + 1) * 512],
                                SACC[:, oc, nt * 512:(nt + 1) * 512],
                                PS[bs + nt][:, :], ALU.add),
                                waits=[t_d2] if nt == 0 else (), sem=DV)
                    t_sde_h[oc % 2] = t_sde
                if stage == "shared" and os.environ.get("SHMB") is not None \
                        and mb == int(os.environ["SHMB"]):
                    t_shs = p.dma("sync", shd_d[:, :, :].rearrange("fc q t -> q fc t"),
                                  SACC[:, :, :], waits=[t_sde], sem=SP, stream="d52")
                    tend = p.dma("sync", out_h.rearrange("(o a) b -> o (a b)", o=1),
                                 shd_d.rearrange("(o f) q t -> o (f q t)", o=1),
                                 waits=[t_shs], sem=ST, stream="d53")
                    p.wait("sync", tend)
                    p4w.close()
                    p4.close()
                    block = ctx.enter_context(nc.Block())
                    p.realize(block)
                    return nc

            if stage == "xbf":
                t_shs = p.dma("gpsimd", shd_d[:, :, :].rearrange("fc q t -> q fc t"),
                              XBF[:, :, :], waits=[t_xbf], sem=SP, stream="d54")
                tend = p.dma("sync", out_h.rearrange("(o a) b -> o (a b)", o=1),
                             shd_d.rearrange("(o f) q t -> o (f q t)", o=1),
                             waits=[t_shs], sem=ST, stream="d55")
                p.wait("sync", tend)
                p4w.close()
                p4.close()
                block = ctx.enter_context(nc.Block())
                p.realize(block)
                return nc
            if stage == "shared":
                t_shs = p.dma("sync", shd_d[:, :, :].rearrange("fc q t -> q fc t"),
                              SACC[:, :, :], waits=[t_sde], sem=SP, stream="d56")
                tend = p.dma("sync", out_h.rearrange("(o a) b -> o (a b)", o=1),
                             shd_d.rearrange("(o f) q t -> o (f q t)", o=1),
                             waits=[t_shs], sem=ST, stream="d57")
                p.wait("sync", tend)
                p4w.close()
                p4.close()
                block = ctx.enter_context(nc.Block())
                p.realize(block)
                return nc

            g5 = gate()
            p4w.close()
            SHTM = p4.enter_context(nc.sbuf_tensor("SHTM", [128, D], F32))
            G1 = p4.enter_context(nc.sbuf_tensor("G1", [128, D], F32))
            G2 = p4.enter_context(nc.sbuf_tensor("G2", [128, D], F32))
            RTD = p4.enter_context(nc.sbuf_tensor("RTD", [128, D], F32))
            SRT = p4.enter_context(nc.sbuf_tensor("SRT", [128, D], F32))
            XRT = p4.enter_context(nc.sbuf_tensor("XRT", [128, D], F32))
            OUTT = p4.enter_context(nc.sbuf_tensor("OUTT", [128, D], F32))
            t_g12 = None
            t_oev = None
            t_osp = None
            t_ftr = None
            for i in range(NT):
                t_g12 = p.op("gpsimd", lambda e, i=i: e.indirect_dma_start(
                    out=G1[:, :], out_offset=None, in_=ag4_out[:, :],
                    in_offset=IndirectOffsetOnAxis(ap=IDX1[:, i:i + 1], axis=0)),
                    waits=(g5 + [t_ag4] if i == 0 else [t_g12]) + ([t_oev] if t_oev else []),
                    sem=p.ssem("ind6"), amt=16, chain=False)
                t_g12 = p.op("gpsimd", lambda e, i=i: e.indirect_dma_start(
                    out=G2[:, :], out_offset=None, in_=ag4_out[:, :],
                    in_offset=IndirectOffsetOnAxis(ap=IDX2[:, i:i + 1], axis=0)),
                    sem=p.ssem("ind7"), amt=16, chain=False)
                t_lds2 = p.dma("sync", SRT[:, :], stm_d[i * 128:(i + 1) * 128, :],
                               waits=[t_oev] if t_oev else g5, sem=SP, stream="d58")
                t_ldx3 = p.dma("sync", XRT[:, :], x_h[i * 128 * 0 + 0:0 + 128, :]
                               if False else
                               x_h.rearrange("(i q) f -> i q f", q=128)[i, :, :],
                               sem=SP, stream="d59")
                for fc in range(FC):
                    t_ftr = te(lambda e, fc=fc, i=i: e.transpose(
                        PS[fc // 4][:, (fc % 4) * 128:(fc % 4) * 128 + 128],
                        SACC[:, fc, i * 128:(i + 1) * 128], id32[:, :]),
                        waits=[t_sde, t_oev] if fc == 0 else (),
                        sem=PE if fc == FC - 1 else None)
                ac(lambda e: e.copy(SHTM[:, 0:512], PS[0][:, :]),
                   waits=[t_ftr] + ([t_oev] if t_oev else []))
                t_she = ac(lambda e: e.copy(SHTM[:, 512:1024], PS[1][:, :]), sem=ACT_S)
                dv(lambda e, i=i: e.tensor_scalar(RTD[:, :], G1[:, :], W1[:, i:i + 1],
                                                  None, ALU.mult),
                   waits=[t_g12, t_she])
                dv(lambda e, i=i: e.scalar_tensor_tensor(RTD[:, :], G2[:, :], W2[:, i:i + 1],
                                                         RTD[:, :], ALU.mult, ALU.add))
                t_ffn = dv(lambda e: e.tensor_tensor(RTD[:, :], RTD[:, :], SHTM[:, :], ALU.add),
                           sem=DV)
                t_gsl = ac(lambda e: e.activation(RTD[:, :], RTD[:, :], AF.Silu),
                           waits=[t_ffn], sem=ACT_S)
                dv(lambda e: e.tensor_tensor(OUTT[:, :], RTD[:, :], SRT[:, :], ALU.mult),
                   waits=[t_gsl, t_lds2, t_ldx3, t_osp])
                t_oev = dv(lambda e: e.tensor_tensor(OUTT[:, :], OUTT[:, :], XRT[:, :], ALU.add),
                           sem=DV)
                t_osp = p.dma("sync", out_h[i * 128:(i + 1) * 128, :], OUTT[:, :],
                              waits=[t_oev], sem=ST, stream="d60")
            p.wait("sync", t_osp)
            p4.close()
        else:
            raise NotImplementedError(stage)

        block = ctx.enter_context(nc.Block())
        p.realize(block)
    return nc


def _host_prep(inputs):
    x = np.asarray(inputs["L0_attn_x"], np.float32)
    vec_names = ["L0_attn_ln1_g", "L0_attn_ln1_b", "L0_attn_ln2_g", "L0_attn_ln2_b",
                 "L0_attn_mix_k", "L0_attn_mix_v", "L0_attn_mix_r",
                 "L0_attn_time_decay", "L0_attn_time_first"]
    vecs = np.stack([np.asarray(inputs[n], np.float32).reshape(FC, 128).T
                     for n in vec_names], axis=2)
    shared = dict(
        wk=np.asarray(inputs["L0_attn_Wk"], np.float32),
        wv=np.asarray(inputs["L0_attn_Wv"], np.float32),
        wr=np.asarray(inputs["L0_attn_Wr"], np.float32),
        wo=np.asarray(inputs["L0_attn_Wo"], np.float32),
        wfk=np.asarray(inputs["L0_attn_Wfk"], np.float32),
        wfv=np.asarray(inputs["L0_attn_Wfv"], np.float32),
        wfr=np.asarray(inputs["L0_attn_Wfr"], np.float32),
        sg=np.asarray(inputs["L0_ffn_sg"], np.float32),
        su=np.asarray(inputs["L0_ffn_su"], np.float32),
        sd=np.asarray(inputs["L0_ffn_sd"], np.float32),
        rwt=np.ascontiguousarray(np.asarray(inputs["L0_ffn_router_w"], np.float32).T),
        vecs=vecs,
        ident32=np.eye(128, dtype=np.float32),
        ident8=np.eye(8, dtype=np.float32),
        identbf=np.eye(128, dtype=ml_dtypes.bfloat16),
    )
    rg = np.asarray(inputs["L0_ffn_rg"], np.float32)
    ru = np.asarray(inputs["L0_ffn_ru"], np.float32)
    rd = np.asarray(inputs["L0_ffn_rd"], np.float32)
    pp = np.arange(128, dtype=np.uint32)
    per_core = []
    for c in range(NCORE):
        b, h = c >> 1, c & 1
        im = dict(shared)
        im["x"] = np.ascontiguousarray(x[b, h * T:(h + 1) * T, :])
        im["rg"] = np.ascontiguousarray(rg[c])
        im["ru"] = np.ascontiguousarray(ru[c])
        im["rd"] = np.ascontiguousarray(rd[c])
        aux = np.zeros((128, 8), np.float32)
        aux[:, 0] = 1.0 - h
        aux[:, 1] = float(h)
        aux[:, 2] = float(c * T)
        im["aux"] = aux
        a8 = np.zeros((8, 8), np.float32)
        a8[:, 0] = (np.arange(8) < c).astype(np.float32)
        a8[:, 1] = (np.arange(8) == c).astype(np.float32)
        a8[:, 2] = np.arange(8, dtype=np.float32) * CAP
        im["aux8"] = a8
        pi = np.zeros((128, FC), np.uint32)
        base = (c - 1) * 1024 if c > 0 else 0
        for fc in range(FC):
            pi[:, fc] = base + fc * 128 + pp
        im["previdx"] = pi
        pi2 = np.zeros((128, FC, 2), np.uint32)
        base2 = (c - 1) * 2048 if c > 0 else 0
        for fc in range(FC):
            pi2[:, fc, 0] = base2 + fc * 128 + pp
            pi2[:, fc, 1] = base2 + 1024 + fc * 128 + pp
        im["previdx2"] = pi2
        per_core.append(im)
    return per_core


def kernel(**inputs):
    per_core = _host_prep(inputs)
    nc = build(_STAGE)
    res = run_bass_kernel_spmd(nc, per_core, core_ids=list(range(NCORE)))
    outs = [np.asarray(r["out"]) for r in res.results]
    return np.stack(outs).reshape(4, 4096, 1024)


# version 22
# speedup vs baseline: 1.0173x; 1.0173x over previous
import os
import numpy as np
import ml_dtypes
from contextlib import ExitStack

import concourse.bass as bass
import concourse.mybir as mybir
from concourse.bass_utils import run_bass_kernel_spmd
from bass_rust import IndirectOffsetOnAxis

F32 = mybir.dt.float32
BF16 = mybir.dt.bfloat16
U32 = mybir.dt.uint32
AF = mybir.ActivationFunctionType
ALU = mybir.AluOpType
AX = mybir.AxisListType

NCORE = 8
T = 2048
D = 1024
FC = 8
NT = 16
E = 8
CAP = 2560
DFF = 4096
DSH = 8192
LN_EPS = 1e-5
WKV_EPS = 1e-8
RMS_EPS = float(np.finfo(np.float32).eps)
BIG = float(2 << 29)

_STAGE = "full"

ENGINES = ("sync", "vector", "scalar", "tensor", "gpsimd")


class Sem:
    __slots__ = ("h", "n", "name")

    def __init__(self, handle, name):
        self.h = handle
        self.n = 0
        self.name = name

    def tok(self):
        return (self, self.n)


class Prog:

    def __init__(self, nc):
        self.nc = nc
        self.steps = {e: [] for e in ENGINES}
        self.auto = {}
        self.streams = {}
        self._mkstream = None

    def ssem(self, name):
        if name not in self.streams:
            self.streams[name] = self._mkstream(name)
        return self.streams[name]

    def op(self, eng, fn, waits=(), sem=None, amt=1, chain=True):
        waits = tuple(w for w in waits if w is not None)
        for s, v in waits:
            assert v <= s.n, f"wait on future value of {s.name}: {v} > {s.n}"
        a = self.auto.get(eng) if chain else None
        if a is not None:
            if a.n > 0:
                waits = waits + ((a, a.n),)
            a.n += 1
            self.steps[eng].append((waits, fn, ((a, 1),)))
            return a.tok() if sem is not None else None
        incs = ()
        if sem is not None:
            sem.n += amt
            incs = ((sem, amt),)
        self.steps[eng].append((waits, fn, incs))
        return sem.tok() if sem is not None else None

    def dma(self, eng, out, in_, waits=(), sem=None, stream=None, **kw):
        assert stream is not None, "p.dma requires stream="
        sem = self.ssem(stream)
        def fn(e, out=out, in_=in_, kw=kw):
            return e.dma_start(out=out, in_=in_, **kw)
        return self.op(eng, fn, waits, sem, 16, chain=False)

    def wait(self, eng, *toks):
        toks = tuple(t for t in toks if t is not None)

        def fn(e):
            return e.nop()
        self.steps[eng].append((toks, fn, ()))

    def realize(self, block):
        for eng_name in ENGINES:
            steps = self.steps[eng_name]

            def mk(steps):
                def run(eng):
                    for waits, fn, incs in steps:
                        for s, v in waits:
                            eng.wait_ge(s.h, v)
                        inst = fn(eng)
                        for s, amt in incs:
                            inst.then_inc(s.h, amt)
                return run
            getattr(block, eng_name)(mk(steps))


def build(stage):
    nc = bass.Bass("TRN2", target_bir_lowering=False)
    p = Prog(nc)

    def di(name, shape, dt=F32):
        return nc.dram_tensor(name, shape, dt, kind="ExternalInput").ap()

    x_h = di("x", [T, D])
    wk_h = di("wk", [D, D]); wv_h = di("wv", [D, D])
    wr_h = di("wr", [D, D]); wo_h = di("wo", [D, D])
    wfk_h = di("wfk", [D, DFF]); wfv_h = di("wfv", [DFF, D]); wfr_h = di("wfr", [D, D])
    sg_h = di("sg", [D, DSH]); su_h = di("su", [D, DSH]); sd_h = di("sd", [DSH, D])
    rg_h = di("rg", [D, DFF]); ru_h = di("ru", [D, DFF]); rd_h = di("rd", [DFF, D])
    rwt_h = di("rwt", [D, E])
    vecs_h = di("vecs", [128, FC, 9])
    aux_h = di("aux", [128, 8])
    aux8_h = di("aux8", [8, 8])
    previdx_h = di("previdx", [128, FC], U32)
    previdx2_h = di("previdx2", [128, FC, 2], U32)
    id32_h = di("ident32", [128, 128])
    id8_h = di("ident8", [8, 8])
    idbf_h = di("identbf", [128, 128], BF16)

    out_h = nc.dram_tensor("out", [T, D], F32, kind="ExternalOutput").ap()

    def dd(name, shape, dt=F32):
        return nc.dram_tensor(name, shape, dt).ap()

    xnfm_d = dd("xnfm_d", [FC, 128, T])
    xnbf_d = dd("xnbf_d", [FC, 128, T], BF16)
    ag1_in = dd("ag1_in", [T, D], BF16)
    ag1_out = dd("ag1_out", [NCORE * T, D], BF16)
    r1a_in = dd("r1a_in", [1, D]); r1a_out = dd("r1a_out", [NCORE, D])
    r1b_in = dd("r1b_in", [1, 2 * D]); r1b_out = dd("r1b_out", [NCORE, 2 * D])
    r2_in = dd("r2_in", [1, D]); r2_out = dd("r2_out", [NCORE, D])
    cnt_in = dd("cnt_in", [1, E]); cnt_out = dd("cnt_out", [NCORE, E])
    tbl_in = dd("tbl_in", [E * CAP, 1]); tbl_out = dd("tbl_out", [E * CAP, 1])
    A_d = dd("A_d", [FC, 128, T]); B_d = dd("B_d", [FC, 128, T]); RS_d = dd("RS_d", [FC, 128, T])
    x2_d = dd("x2_d", [FC, 128, T])
    stm_d = dd("stm_d", [T, D])
    myrow_d = dd("myrow_d", [CAP, 1])
    ag4_in = dd("ag4_in", [CAP, D]); ag4_out = dd("ag4_out", [NCORE * CAP, D])
    shd_d = dd("shd_d", [FC, 128, T])

    with ExitStack() as ctx:
        sems = {}
        for name in ("LD", "SP", "DV", "ACT", "PE", "PL", "CC", "ST"):
            sems[name] = Sem(ctx.enter_context(nc.semaphore(name.lower())), name)
        LD, SP, DV, ACT_S, PE, PL, CC, ST = (sems[k] for k in
                                             ("LD", "SP", "DV", "ACT", "PE", "PL", "CC", "ST"))
        for eng_name in ("vector", "scalar"):
            p.auto[eng_name] = Sem(ctx.enter_context(nc.semaphore("au_" + eng_name)),
                                   "AU_" + eng_name)
        AUPL = Sem(ctx.enter_context(nc.semaphore("au_pool")), "AU_PL")
        p._mkstream = lambda name: Sem(ctx.enter_context(nc.semaphore("st_" + name)), name)

        sb = lambda n, s, d=F32: ctx.enter_context(nc.sbuf_tensor(n, s, d))

        id32 = sb("id32", [128, 128])
        id8 = sb("id8", [8, 8])
        idbf = sb("idbf", [128, 128], BF16)
        vecs = sb("s_vecs", [128, FC, 9])
        dcols = sb("dcols", [128, FC, 8])
        auxc = sb("auxc", [128, 8])
        aux8 = sb("s_aux8", [8, 8])
        pidx = sb("pidx", [128, FC], U32)
        pidx2 = sb("pidx2", [128, FC, 2], U32)
        W1 = sb("W1", [128, NT]); W2 = sb("W2", [128, NT])
        IDX1 = sb("IDX1", [128, NT], U32); IDX2 = sb("IDX2", [128, NT], U32)
        COLS = sb("COLS", [128, 64])
        CARR = sb("CARR", [128, FC, 4])
        pmid = ExitStack()
        ramp = pmid.enter_context(nc.sbuf_tensor("ramp", [128, T], F32))
        XN1 = pmid.enter_context(nc.sbuf_tensor("XN1", [128, FC, 8 + T], BF16))

        f0c = auxc[:, 0:1]
        f1c = auxc[:, 1:2]

        def dv(fn, waits=(), sem=None):
            return p.op("vector", fn, waits, sem)

        def ac(fn, waits=(), sem=None):
            return p.op("scalar", fn, waits, sem)

        def te(fn, waits=(), sem=None):
            return p.op("tensor", fn, waits, sem)

        def gp(fn, waits=(), sem=None):
            p.auto["gpsimd"] = AUPL
            r = p.op("gpsimd", fn, waits, sem)
            del p.auto["gpsimd"]
            return r

        def combine2(HMv, mdst, vdst, wait):
            w = HMv.ap().shape[1] if hasattr(HMv, 'ap') else HMv.shape[1]
            a = COLS[:, 32:32 + w]; b2 = COLS[:, 48:48 + w]
            dv(lambda e: e.tensor_tensor(mdst, HMv[:, :, 0], HMv[:, :, 2], ALU.add), waits=[wait])
            dv(lambda e: e.tensor_scalar(mdst, mdst, 0.5, None, ALU.mult))
            dv(lambda e: e.tensor_tensor(a, HMv[:, :, 0], HMv[:, :, 0], ALU.mult))
            dv(lambda e: e.tensor_tensor(a, a, HMv[:, :, 1], ALU.add))
            dv(lambda e: e.tensor_tensor(b2, HMv[:, :, 2], HMv[:, :, 2], ALU.mult))
            dv(lambda e: e.tensor_tensor(b2, b2, HMv[:, :, 3], ALU.add))
            dv(lambda e: e.tensor_tensor(a, a, b2, ALU.add))
            dv(lambda e: e.tensor_scalar(a, a, 0.5, None, ALU.mult))
            dv(lambda e: e.tensor_tensor(b2, mdst, mdst, ALU.mult))
            return dv(lambda e: e.tensor_tensor(vdst, a, b2, ALU.subtract), sem=DV)

        def combine_halves(mdst, vdst, wait):
            a = COLS[:, 0:NT]; b2 = COLS[:, 16:32]
            t0 = dv(lambda e: e.tensor_tensor(mdst, HM[:, :, 0], HM[:, :, 2], ALU.add), waits=[wait])
            dv(lambda e: e.tensor_scalar(mdst, mdst, 0.5, None, ALU.mult))
            dv(lambda e: e.tensor_tensor(a, HM[:, :, 0], HM[:, :, 0], ALU.mult))
            dv(lambda e: e.tensor_tensor(a, a, HM[:, :, 1], ALU.add))
            dv(lambda e: e.tensor_tensor(b2, HM[:, :, 2], HM[:, :, 2], ALU.mult))
            dv(lambda e: e.tensor_tensor(b2, b2, HM[:, :, 3], ALU.add))
            dv(lambda e: e.tensor_tensor(a, a, b2, ALU.add))
            dv(lambda e: e.tensor_scalar(a, a, 0.5, None, ALU.mult))
            dv(lambda e: e.tensor_tensor(b2, mdst, mdst, ALU.mult))
            return dv(lambda e: e.tensor_tensor(vdst, a, b2, ALU.subtract), sem=DV)

        def rstd_batch(dstv, srcv, scale, eps, wait):
            t0 = dv(lambda e: e.tensor_scalar(dstv, srcv, scale, eps, ALU.mult, ALU.add),
                    waits=[wait], sem=DV)
            t1 = ac(lambda e: e.activation(dstv, dstv, AF.Ln), waits=[t0], sem=ACT_S)
            t2 = dv(lambda e: e.tensor_scalar(dstv, dstv, -0.5, None, ALU.mult), waits=[t1], sem=DV)
            return ac(lambda e: e.activation(dstv, dstv, AF.Exp), waits=[t2], sem=ACT_S)

        p01 = ExitStack()
        sbp = lambda n, sh, d=F32: p01.enter_context(nc.sbuf_tensor(n, sh, d))
        rwtc = sbp("rwtc", [128, FC, E])
        LG = sbp("LG", [128, NT, E])
        EQ1 = sbp("EQ1", [128, NT, E]); EQ2 = sbp("EQ2", [128, NT, E])
        MX = sbp("MX", [128, NT, 4])
        KP1 = sbp("KP1", [128, NT]); KP2 = sbp("KP2", [128, NT])
        SLT1 = sbp("SLT1", [128, NT]); SLT2 = sbp("SLT2", [128, NT])
        STATC = sbp("STATC", [128, NT, 6])
        HM = sbp("HM", [128, NT, 4])
        RMSC = sbp("RMSC", [128, NT, 2])
        TM = p01.enter_context(nc.sbuf_tensor("TM", [128, NT, D], F32))
        SQS = p01.enter_context(nc.sbuf_tensor("SQS", [128, D], F32))
        Z1 = p01.enter_context(nc.sbuf_tensor("Z1", [128, D], F32))
        STG = p01.enter_context(nc.sbuf_tensor("STG", [128, D], F32))
        ST6 = p01.enter_context(nc.sbuf_tensor("ST6", [128, 12], F32))
        p0ctx = ExitStack()
        PT0 = p0ctx.enter_context(nc.psum_tensor("PT0", [128, 512], F32))
        PT1 = p0ctx.enter_context(nc.psum_tensor("PT1", [128, 512], F32))
        PT2 = p0ctx.enter_context(nc.psum_tensor("PT2", [128, 512], F32))
        PT3 = p0ctx.enter_context(nc.psum_tensor("PT3", [128, 512], F32))
        PLG = p0ctx.enter_context(nc.psum_tensor("PLG", [128, NT * E], F32))

        t_ldx = p.dma("sync", TM[:, :, :],
                      x_h.rearrange("(i p) f -> p i f", p=128), sem=LD, stream="d0")
        for dst, src in ((id32[:, :], id32_h[:, :]), (id8[:, :], id8_h[:, :]),
                         (idbf[:, :], idbf_h[:, :]), (vecs[:, :, :], vecs_h),
                         (auxc[:, :], aux_h[:, :]), (aux8[:, :], aux8_h[:, :]),
                         (pidx[:, :], previdx_h[:, :]),
                         (pidx2[:, :, :], previdx2_h),
                         (rwtc[:, :, :], rwt_h.rearrange("(fc p) e -> p fc e", p=128))):
            t_lds = p.dma("sync", dst, src, sem=LD, stream="d1")

        t_ramp = gp(lambda e: e.iota(ramp[:, :], [[1, T]], base=1, channel_multiplier=0,
                                     allow_small_or_imprecise_dtypes=True), sem=PL)

        dv(lambda e: e.tensor_scalar(dcols[:, :, 0], vecs[:, :, 4], -1.0, 1.0, ALU.mult, ALU.add), waits=[t_lds])
        dv(lambda e: e.tensor_scalar(dcols[:, :, 1], vecs[:, :, 5], -1.0, 1.0, ALU.mult, ALU.add))
        dv(lambda e: e.tensor_scalar(dcols[:, :, 2], vecs[:, :, 6], -1.0, 1.0, ALU.mult, ALU.add))
        t_dvmix = dv(lambda e: e.tensor_scalar(dcols[:, :, 6], vecs[:, :, 8], f0c, None, ALU.mult), sem=DV)
        t_w = ac(lambda e: e.activation(dcols[:, :, 3], vecs[:, :, 7], AF.Exp), waits=[t_lds], sem=ACT_S)
        t_negw = dv(lambda e: e.tensor_scalar(dcols[:, :, 4], dcols[:, :, 3], -1.0, None, ALU.mult),
                    waits=[t_w], sem=DV)
        t_ew = ac(lambda e: e.activation(dcols[:, :, 5], dcols[:, :, 4], AF.Exp), waits=[t_negw], sem=ACT_S)

        for i in range(NT):
            tile = TM[:, i, :]
            dv(lambda e, t=tile: e.bn_stats(ST6[:, 0:6], t[:, 0:512]), waits=[t_ldx, t_lds, t_ramp, t_ew, t_dvmix] if i == 0 else ())
            dv(lambda e, t=tile: e.bn_stats(ST6[:, 6:12], t[:, 512:1024]))
            dv(lambda e, i=i: e.bn_aggr(HM[:, i, 0:2], ST6[:, 0:6]))
            last = dv(lambda e, i=i: e.bn_aggr(HM[:, i, 2:4], ST6[:, 6:12]),
                      sem=DV if i == NT - 1 else None)
        t_stats = last
        t_stats = combine_halves(STATC[:, :, 0], STATC[:, :, 1], t_stats)

        t_rstd0 = rstd_batch(STATC[:, :, 2], STATC[:, :, 1], 1.0, LN_EPS, t_stats)

        for i in range(NT):
            tile = TM[:, i, :]
            dv(lambda e, t=tile, i=i: e.tensor_scalar(
                t, t, STATC[:, i, 0:1], STATC[:, i, 2:3], ALU.subtract, ALU.mult),
                waits=[t_rstd0] if i == 0 else ())
            dv(lambda e, t=tile: e.bn_stats(ST6[:, 0:6], t[:, 0:512]))
            dv(lambda e, t=tile: e.bn_stats(ST6[:, 6:12], t[:, 512:1024]))
            dv(lambda e, i=i: e.bn_aggr(HM[:, i, 0:2], ST6[:, 0:6]))
            t_xn = dv(lambda e, i=i: e.bn_aggr(HM[:, i, 2:4], ST6[:, 6:12]),
                      sem=DV if i == NT - 1 else None)

        for i in range(NT):
            tile = TM[:, i, :]
            t_rms = ac(lambda e, t=tile, i=i: e.activation(SQS[:, :], t, AF.Square,
                                                           accum_out=RMSC[:, i, 0:1]),
                       waits=[t_xn] if i == 0 else (), sem=ACT_S if i == NT - 1 else None)
        t_rrstd = rstd_batch(RMSC[:, :, 1], RMSC[:, :, 0], 1.0 / D, RMS_EPS, t_rms)

        t_ship = p.dma("gpsimd", ag1_in.rearrange("(i p) f -> p i f", p=128),
                       TM[:, :, :], waits=[t_xn], sem=SP, stream="d2")
        t_ag1 = p.op("gpsimd", chain=False, fn=lambda e: e.collective_compute(
            "AllGather", ALU.bypass, replica_groups=[list(range(NCORE))],
            ins=[ag1_in[:, :]], outs=[ag1_out[:, :]]), waits=[t_ship], sem=CC)

        t_m1v1 = combine_halves(STATC[:, :, 3], STATC[:, :, 4], t_xn)
        t_rstd1 = rstd_batch(STATC[:, :, 5], STATC[:, :, 4], 1.0, LN_EPS, t_m1v1)
        t_pad = dv(lambda e: e.memset(XN1[:, :, 0:8], 0.0), sem=DV)

        t_evx = None
        t_evz = None
        t_lgt = None
        t_sp = None
        t_sp3 = None
        t_ztr = None
        for i in range(NT):
            tile = TM[:, i, :]
            tz = dv(lambda e, t=tile, i=i: e.tensor_scalar(
                Z1[:, :], t, STATC[:, i, 3:4], STATC[:, i, 5:6], ALU.subtract, ALU.mult),
                waits=[t_rstd1 if i == 0 else None, t_ztr], sem=DV)
            for fc in range(FC):
                pt = PT0 if fc < 4 else PT1
                te(lambda e, t=tile, fc=fc, pt=pt: e.transpose(
                    pt[:, (fc % 4) * 128:(fc % 4) * 128 + 128],
                    t[:, fc * 128:fc * 128 + 128], id32[:, :]),
                    waits=(tz, t_evx, t_evz) if fc == 0 else ())
            for fc in range(FC):
                pt = PT2 if fc < 4 else PT3
                t_ztr = te(lambda e, fc=fc, pt=pt: e.transpose(
                    pt[:, (fc % 4) * 128:(fc % 4) * 128 + 128],
                    Z1[:, fc * 128:fc * 128 + 128], id32[:, :]),
                    waits=[tz] if fc == 0 else (), sem=PE if fc == FC - 1 else None)
            ac(lambda e: e.copy(STG[:, 0:512], PT0[:, :]), waits=(t_ztr, t_sp3, t_sp, t_lgt))
            t_evx = ac(lambda e: e.copy(STG[:, 512:1024], PT1[:, :]), sem=ACT_S)
            t_sp3 = p.dma("sync", xnfm_d[:, :, i * 128:(i + 1) * 128].rearrange("fc p t -> p fc t"),
                  STG.ap().rearrange("p (fc t) -> p fc t", fc=FC), waits=[t_evx], sem=SP, stream="d3")
            t_sp = p.dma("gpsimd", xnbf_d[:, :, i * 128:(i + 1) * 128].rearrange("fc p t -> p fc t"),
                         STG.ap().rearrange("p (fc t) -> p fc t", fc=FC), waits=[t_evx], sem=SP, stream="d4")
            for fc in range(FC):
                t_lgt = te(lambda e, fc=fc, i=i: e.matmul(
                    PLG[:, i * E:(i + 1) * E], STG[:, fc * 128:(fc + 1) * 128],
                    rwtc[:, fc, :], start=(fc == 0), stop=(fc == FC - 1)),
                    waits=[t_evx, t_lds] if fc == 0 else (), sem=PE if fc == FC - 1 else None)
            for fc in range(FC):
                pt = PT2 if fc < 4 else PT3
                t_evz = dv(lambda e, fc=fc, pt=pt, i=i: e.tensor_scalar(
                    XN1[:, fc, 8 + i * 128:8 + (i + 1) * 128],
                    pt[:, (fc % 4) * 128:(fc % 4) * 128 + 128],
                    vecs[:, fc, 0:1], vecs[:, fc, 1:2], ALU.mult, ALU.add),
                    waits=(t_ztr, t_pad) if fc == 0 else (), sem=DV if fc == FC - 1 else None)
        t_xn1 = t_evz

        t_lg = None
        for i in range(NT):
            t_lg = dv(lambda e, i=i: e.tensor_scalar(
                LG[:, i, :], PLG[:, i * E:(i + 1) * E], RMSC[:, i, 1:2], None, ALU.mult),
                waits=(t_lgt, t_rrstd) if i == 0 else (), sem=DV if i == NT - 1 else None)

        PR1 = p0ctx.enter_context(nc.psum_tensor("PR1", [1, 1024], BF16))
        SR1 = p01.enter_context(nc.sbuf_tensor("SR1", [1, 1024], F32))
        PRV = p01.enter_context(nc.sbuf_tensor("PRV", [128, FC], F32))
        for fc in range(FC):
            t_tr1 = te(lambda e, fc=fc: e.transpose(
                PR1[:, fc * 128:(fc + 1) * 128], XN1[:, fc, 7 + T:8 + T], idbf[:, :]),
                waits=[t_xn1] if fc == 0 else (), sem=PE if fc == FC - 1 else None)
        t_evr1 = dv(lambda e: e.tensor_copy(SR1[:, :], PR1[:, :]), waits=[t_tr1], sem=DV)
        t_pub1 = p.dma("sync", r1a_in[:, :], SR1[:, :], waits=[t_evr1], sem=SP, stream="d5")
        t_r1a = p.op("gpsimd", lambda e: e.collective_compute(
            "AllGather", ALU.bypass, replica_groups=[list(range(NCORE))],
            ins=[r1a_in[:, :]], outs=[r1a_out[:, :]]), waits=[t_pub1], sem=CC)
        for fc in range(FC):
            t_prv = p.op("gpsimd", lambda e, fc=fc: e.indirect_dma_start(
                out=PRV[:, fc:fc + 1], out_offset=None,
                in_=r1a_out.rearrange("c (d o) -> (c d) o", o=1),
                in_offset=IndirectOffsetOnAxis(ap=pidx[:, fc:fc + 1], axis=0)),
                waits=[t_r1a] if fc == 0 else (), sem=p.ssem("ind1"), amt=16)
        for fc in range(FC):
            t_prvw = dv(lambda e, fc=fc: e.tensor_scalar(
                XN1[:, fc, 7:8], PRV[:, fc:fc + 1], f1c, None, ALU.mult),
                waits=[t_prv] if fc == 0 else (), sem=DV if fc == FC - 1 else None)

        p0ctx.close()
        SEL8 = p01.enter_context(nc.sbuf_tensor("SEL8", [8, T], F32))
        CUM8 = p01.enter_context(nc.sbuf_tensor("CUM8", [8, T], F32))
        Q8 = p01.enter_context(nc.sbuf_tensor("Q8", [8, T], F32))
        KP8 = p01.enter_context(nc.sbuf_tensor("KP8", [8, T], F32))
        SL8 = p01.enter_context(nc.sbuf_tensor("SL8", [8, T], F32))
        EQ8 = p01.enter_context(nc.sbuf_tensor("EQ8", [8, 2 * T], F32))
        ON8 = p01.enter_context(nc.sbuf_tensor("ON8", [8, T], F32))
        SC8 = p01.enter_context(nc.sbuf_tensor("SC8", [8, 64], F32))
        TKID = p01.enter_context(nc.sbuf_tensor("TKID", [128, NT], F32))
        SID1 = p01.enter_context(nc.sbuf_tensor("SID1", [128, NT], U32))
        SID2 = p01.enter_context(nc.sbuf_tensor("SID2", [128, NT], U32))
        TMPT = p01.enter_context(nc.sbuf_tensor("TMPT", [128, NT, E], F32))
        ZROW = p01.enter_context(nc.sbuf_tensor("ZROW", [128, E * CAP // 128], F32))
        P8A = p01.enter_context(nc.psum_tensor("P8A", [8, T], F32))
        P8B = p01.enter_context(nc.psum_tensor("P8B", [128, NT * E], F32))

        t_m1 = dv(lambda e: e.tensor_reduce(MX[:, :, 0], LG[:, :, :], AX.X, ALU.max),
                  waits=[t_lg], sem=DV)
        for i in range(NT):
            dv(lambda e, i=i: e.tensor_scalar(EQ1[:, i, :], LG[:, i, :], MX[:, i, 0:1],
                                              None, ALU.is_equal),
               waits=[t_m1] if i == 0 else ())
        t_eq1 = dv(lambda e: e.scalar_tensor_tensor(TMPT[:, :, :], EQ1[:, :, :], -BIG,
                                                    LG[:, :, :], ALU.mult, ALU.add), sem=DV)
        t_m2 = dv(lambda e: e.tensor_reduce(MX[:, :, 1], TMPT[:, :, :], AX.X, ALU.max),
                  waits=[t_eq1], sem=DV)
        for i in range(NT):
            t_eq2 = dv(lambda e, i=i: e.tensor_scalar(EQ2[:, i, :], TMPT[:, i, :], MX[:, i, 1:2],
                                                      None, ALU.is_equal),
                       waits=[t_m2] if i == 0 else (), sem=DV if i == NT - 1 else None)
        dv(lambda e: e.tensor_tensor(MX[:, :, 2], MX[:, :, 0], MX[:, :, 1], ALU.subtract),
           waits=[t_m2])
        t_d = dv(lambda e: e.tensor_scalar(MX[:, :, 3], MX[:, :, 2], -1.0, None, ALU.mult), sem=DV)
        t_rw = ac(lambda e: e.activation(MX[:, :, 2:4], MX[:, :, 2:4], AF.Sigmoid),
                  waits=[t_d], sem=ACT_S)

        t_sel = dv(lambda e: e.tensor_tensor(TMPT[:, :, :], EQ1[:, :, :], EQ2[:, :, :], ALU.add),
                   waits=[t_eq2], sem=DV)
        for i in range(NT):
            t_trs = te(lambda e, i=i: e.transpose(
                P8A[:, i * 128:(i + 1) * 128], TMPT[:, i, :], id32[:, :]),
                waits=[t_sel] if i == 0 else (), sem=PE if i == NT - 1 else None)
        t_sel8 = dv(lambda e: e.tensor_copy(SEL8[:, :], P8A[:, :]), waits=[t_trs], sem=DV)
        dv(lambda e: e.memset(ON8[:, :], 1.0))
        t_cum = dv(lambda e: e.tensor_tensor_scan(CUM8[:, :], ON8[:, :], SEL8[:, :],
                                                  0.0, ALU.mult, ALU.add), sem=DV)
        t_cnt_tr = te(lambda e: e.transpose(P8A[0:1, 0:8], CUM8[:, T - 1:T], id8[:, :]),
                      waits=[t_cum], sem=PE)
        t_cnt = dv(lambda e: e.tensor_copy(SC8[0:1, 0:8], P8A[0:1, 0:8]), waits=[t_cnt_tr], sem=DV)
        t_pubc = p.dma("sync", cnt_in[:, :], SC8[0:1, 0:8], waits=[t_cnt], sem=SP, stream="d6")
        t_agc = p.op("gpsimd", lambda e: e.collective_compute(
            "AllGather", ALU.bypass, replica_groups=[list(range(NCORE))],
            ins=[cnt_in[:, :]], outs=[cnt_out[:, :]]), waits=[t_pubc], sem=CC)
        t_ldc = p.dma("sync", SC8[:, 8:16], cnt_out[:, :], waits=[t_agc], sem=LD, stream="d7")
        t_msk = dv(lambda e: e.tensor_scalar(SC8[:, 16:24], SC8[:, 8:16], aux8[:, 0:1],
                                             None, ALU.mult), waits=[t_ldc], sem=DV)
        t_otr = te(lambda e: e.transpose(P8A[0:8, 8:16], SC8[:, 16:24], id8[:, :]),
                   waits=[t_msk], sem=PE)
        t_offs = dv(lambda e: e.tensor_reduce(SC8[:, 24:25], P8A[:, 8:16], AX.X, ALU.add),
                    waits=[t_otr], sem=DV)

        dv(lambda e: e.tensor_tensor(Q8[:, :], CUM8[:, :], SEL8[:, :], ALU.subtract),
           waits=[t_offs])
        dv(lambda e: e.tensor_scalar(Q8[:, :], Q8[:, :], SC8[:, 24:25], None, ALU.add))
        dv(lambda e: e.tensor_scalar(KP8[:, :], Q8[:, :], float(CAP), None, ALU.is_lt))
        dv(lambda e: e.tensor_tensor(KP8[:, :], KP8[:, :], SEL8[:, :], ALU.mult))
        t_slot = dv(lambda e: e.tensor_scalar(SL8[:, :], Q8[:, :], aux8[:, 2:3], None, ALU.add),
                    sem=DV)

        PR8 = p01.enter_context(nc.sbuf_tensor("PR8", [8, T], F32))
        t_eqem = t_slot
        for k, (eqsrc, dstoff) in enumerate(((EQ1, 0), (EQ2, T))):
            for i in range(NT):
                t_tre = te(lambda e, i=i, eqsrc=eqsrc: e.transpose(
                    P8A[:, i * 128:(i + 1) * 128], eqsrc[:, i, :], id32[:, :]),
                    waits=[t_eqem] if i == 0 else (), sem=PE if i == NT - 1 else None)
            t_eqem = dv(lambda e, dstoff=dstoff: e.tensor_copy(
                EQ8[:, dstoff:dstoff + T], P8A[:, :]), waits=[t_tre], sem=DV)

        def pull_tm(eq_off, Xsrc, dst, wait_tok):
            t_pr = dv(lambda e: e.tensor_tensor(PR8[:, :], EQ8[:, eq_off:eq_off + T],
                                                Xsrc[:, :], ALU.mult),
                      waits=[wait_tok], sem=DV)
            for i in range(NT):
                t_trp = te(lambda e, i=i: e.transpose(
                    P8B[:, i * 8:(i + 1) * 8], PR8[:, i * 128:(i + 1) * 128], id8[:, :]),
                    waits=[t_pr] if i == 0 else (), sem=PE if i == NT - 1 else None)
            return dv(lambda e: e.tensor_reduce(
                dst, P8B.ap().rearrange("p (i e) -> p i e", e=E), AX.X, ALU.add),
                waits=[t_trp], sem=DV)

        t_k1 = pull_tm(0, KP8, KP1[:, :], t_eqem)
        t_k2 = pull_tm(T, KP8, KP2[:, :], t_k1)
        t_s1 = pull_tm(0, SL8, SLT1[:, :], t_k2)
        t_s2 = pull_tm(T, SL8, SLT2[:, :], t_s1)

        dv(lambda e: e.tensor_tensor(W1[:, :], MX[:, :, 2], KP1[:, :], ALU.mult), waits=[t_s2, t_rw])
        dv(lambda e: e.tensor_tensor(W2[:, :], MX[:, :, 3], KP2[:, :], ALU.mult))
        dv(lambda e: e.tensor_tensor(SQS[:, 16:32], SLT1[:, :], KP1[:, :], ALU.mult))
        dv(lambda e: e.tensor_copy(IDX1[:, :], SQS[:, 16:32]))
        dv(lambda e: e.tensor_tensor(SQS[:, 16:32], SLT2[:, :], KP2[:, :], ALU.mult))
        dv(lambda e: e.tensor_copy(IDX2[:, :], SQS[:, 16:32]))
        dv(lambda e: e.tensor_scalar(SQS[:, 0:NT], KP1[:, :], -BIG, BIG, ALU.mult, ALU.add))
        dv(lambda e: e.tensor_tensor(SQS[:, 0:NT], SQS[:, 0:NT], SLT1[:, :], ALU.add))
        dv(lambda e: e.tensor_copy(SID1[:, :], SQS[:, 0:NT]))
        dv(lambda e: e.tensor_scalar(SQS[:, 0:NT], KP2[:, :], -BIG, BIG, ALU.mult, ALU.add))
        dv(lambda e: e.tensor_tensor(SQS[:, 0:NT], SQS[:, 0:NT], SLT2[:, :], ALU.add))
        t_sidx = dv(lambda e: e.tensor_copy(SID2[:, :], SQS[:, 0:NT]), sem=DV)

        t_io = gp(lambda e: e.iota(TKID[:, :], [[128, NT]], base=0, channel_multiplier=1,
                                   allow_small_or_imprecise_dtypes=True), sem=PL)
        t_tk = dv(lambda e: e.tensor_scalar(TKID[:, :], TKID[:, :], auxc[:, 2:3], None,
                                            ALU.add), waits=[t_io], sem=DV)
        t_zr = dv(lambda e: e.memset(ZROW[:, :], 0.0), sem=DV)
        t_ztb = p.dma("sync", tbl_in.rearrange("(px x) o -> px (x o)", px=128),
                      ZROW[:, 0:E * CAP // 128], waits=[t_zr], sem=SP, stream="d8")
        t_sc = None
        for sid in (SID1, SID2):
            for i in range(NT):
                t_sc = p.op("gpsimd", lambda e, sid=sid, i=i: e.indirect_dma_start(
                    out=tbl_in[:, :],
                    out_offset=IndirectOffsetOnAxis(ap=sid[:, i:i + 1], axis=0),
                    in_=TKID[:, i:i + 1], in_offset=None,
                    bounds_check=E * CAP - 1, oob_is_err=False),
                    waits=(t_sidx, t_tk, t_ztb) if (sid is SID1 and i == 0) else (),
                    sem=p.ssem("ind2"), amt=16)
        t_ar = p.op("gpsimd", lambda e: e.collective_compute(
            "AllReduce", ALU.add, replica_groups=[list(range(NCORE))],
            ins=[tbl_in[:, :]], outs=[tbl_out[:, :]]), waits=[t_sc], sem=CC)

        if stage == "stats":
            t0 = p.dma("sync", out_h[16:28, :].rearrange("(o q) d -> o (q d)", o=1).rearrange("o (q x) -> q (o x)", q=128),
                       STATC[:, :, :], waits=[t_rstd0], sem=ST, stream="d9")
            t0b = p.dma("sync", out_h[28:40, :].rearrange("(o q) d -> o (q d)", o=1).rearrange("o (q x) -> q (o x)", q=128),
                       STATC[:, :, :], waits=[t_xn], sem=ST, stream="d10")
            t1 = p.dma("sync", out_h[0:12, :].rearrange("(o q) d -> o (q d)", o=1).rearrange("o (q x) -> q (o x)", q=128),
                       STATC[:, :, :], waits=[t_rstd1], sem=ST, stream="d11")
            t2 = p.dma("sync", out_h[12:16, :].rearrange("(o q) d -> o (q d)", o=1).rearrange("o (q x) -> q (o x)", q=128),
                       RMSC[:, :, :], waits=[t_rrstd], sem=ST, stream="d12")
            p.wait("sync", t1, t2)
        elif stage == "xn":
            tend = p.dma("sync", out_h.rearrange("(o a) b -> o (a b)", o=1),
                         xnfm_d.rearrange("(o f) p t -> o (f p t)", o=1), waits=[p.ssem("d3").tok(), p.ssem("d4").tok()], sem=ST, stream="d13")
            p.wait("sync", tend)
        elif stage == "xn1":
            tend = p.dma("gpsimd", out_h.rearrange("(p r) d -> p (r d)", p=128),
                         XN1[:, :, 8:8 + T], waits=[t_xn1, t_prvw], sem=ST, stream="d14")
            p.wait("gpsimd", tend)
        elif stage == "router":
            toks = [t_ar]
            row = 0

            def dump(src_ap, waits):
                nonlocal row
                if len(src_ap.shape) == 3:
                    src_ap = src_ap.rearrange("p a b -> p (a b)")
                P_, rest = src_ap.shape[0], src_ap.shape[1]
                total = P_ * rest
                nrow = (total + 1023) // 1024
                if total % 1024 == 0:
                    dst = out_h[row:row + nrow, :]
                else:
                    dst = out_h[row:row + 1, 0:total]
                dst = dst.rearrange("(o a) b -> o (a b)", o=1).rearrange("o (q x) -> q (o x)", q=P_)
                tt = p.dma("sync", dst, src_ap, waits=waits, sem=ST, stream="d15")
                row += nrow
                toks.append(tt)
            dump(LG.ap(), [t_lg])
            dump(MX.ap(), [t_rw])
            for t8, tok in ((SEL8, t_eqem), (CUM8, t_cum), (Q8, t_slot),
                            (KP8, t_slot), (SL8, t_slot)):
                dump(t8.ap(), [tok])
            for ttm, tok in ((KP1, t_k1), (KP2, t_k2), (SLT1, t_s1), (SLT2, t_s2),
                             (W1, t_sidx), (W2, t_sidx), (TKID, t_tk)):
                dump(ttm.ap(), [tok])
            dump(tbl_out.rearrange("(r d) o -> r (d o)", d=1024), [t_ar])
            nonlocal_row = row
            tt1 = p.dma("gpsimd", out_h[nonlocal_row:nonlocal_row + 2, :].rearrange(
                "(o a) b -> o (a b)", o=1).rearrange("o (q x) -> q (o x)", q=128),
                SID1[:, :], waits=[t_sidx], sem=ST, stream="d16")
            row += 2
            toks.append(tt1)
            dump(TKID.ap(), [t_tk])
            dump(SC8[:, 0:32], [t_offs])
            p.wait("sync", *toks)
        elif stage in ("s", "wkv", "x2", "yn", "shared", "xbf", "hb", "full"):
            TTW = 512
            NTT = T // TTW

            def gate():
                return [(p.auto["vector"], p.auto["vector"].n),
                        (p.auto["scalar"], p.auto["scalar"].n),
                        PE.tok(), PL.tok()] + [sm.tok() for sm in p.streams.values()]

            g0 = gate()
            p01.close()
            pA = ExitStack()
            sba = lambda n, sh, d=F32: pA.enter_context(nc.sbuf_tensor(n, sh, d))
            WKb = sba("WKb", [128, FC, D], BF16)
            WVb = sba("WVb", [128, FC, D], BF16)
            WRb = sba("WRb", [128, FC, D], BF16)
            MIXb = sba("MIXb", [128, FC, TTW], BF16)
            EKU = sba("EKU", [128, FC, TTW])
            ASC = sba("ASC", [128, FC, TTW])
            BSC = sba("BSC", [128, FC, TTW])
            RSS = sba("RSS", [128, FC, TTW])
            EWT_A = sba("EWT_A", [128, TTW])
            ONESW = sba("ONESW", [128, TTW])
            TMPA = sba("TMPA", [128, TTW])
            SR1b = sba("SR1b", [1, 2 * D])
            PRV2 = sba("PRV2", [128, FC, 2])
            PA = [pA.enter_context(nc.psum_tensor(f"PA{j}", [128, 512], F32))
                  for j in range(8)]

            t_wk = p.dma("gpsimd", WKb[:, :, :], wk_h.rearrange("(kc q) o -> q kc o", q=128),
                         waits=g0, sem=LD, stream="d17")
            t_wv = p.dma("gpsimd", WVb[:, :, :], wv_h.rearrange("(kc q) o -> q kc o", q=128), sem=LD, stream="d18")
            t_wr = p.dma("gpsimd", WRb[:, :, :], wr_h.rearrange("(kc q) o -> q kc o", q=128), sem=LD, stream="d19")

            dv(lambda e: e.memset(ONESW[:, :], 1.0), waits=g0)
            t_carr0 = dv(lambda e: e.memset(CARR[:, :, 0:2], 0.0), sem=DV)

            t_kpe = t_vpe = t_rpe = None
            t_ekev = t_uall = t_rsev = None
            t_spill = t_spA = t_spB = None
            for tt in range(NTT):
                cur = slice(8 + tt * TTW, 8 + (tt + 1) * TTW)
                prv = slice(7 + tt * TTW, 7 + (tt + 1) * TTW)

                def mixes(mixcol, omixcol, wait0, cur=cur, prv=prv):
                    tmix = None
                    for fc in range(FC):
                        dv(lambda e, fc=fc, prv=prv, omixcol=omixcol: e.tensor_scalar(
                            TMPA[:, :], XN1[:, fc, prv], dcols[:, fc, omixcol:omixcol + 1], None, ALU.mult),
                           waits=[wait0] if fc == 0 else ())
                        tmix = dv(lambda e, fc=fc, cur=cur, mixcol=mixcol: e.scalar_tensor_tensor(
                            MIXb[:, fc, :], XN1[:, fc, cur], vecs[:, fc, mixcol:mixcol + 1],
                            TMPA[:, :], ALU.mult, ALU.add), sem=DV)
                    return tmix

                t_mk = mixes(4, 0, t_rpe or t_carr0)
                for mc in range(FC):
                    for kc in range(FC):
                        t_kpe = te(lambda e, mc=mc, kc=kc: e.matmul(
                            PA[mc][:, :], WKb[:, kc, mc * 128:(mc + 1) * 128], MIXb[:, kc, :],
                            start=(kc == 0), stop=(kc == FC - 1)),
                            waits=[t_mk, t_wk, t_rsev] if (mc == 0 and kc == 0) else (),
                            sem=PE if kc == FC - 1 else None)
                t_ek_mc = []
                for mc in range(FC):
                    if tt == 0:
                        ac(lambda e, mc=mc: e.activation(
                            EKU[:, mc, 0:1], PA[mc][:, 0:1], AF.Exp, bias=dcols[:, mc, 6:7]),
                           waits=[t_kpe, t_uall] if mc == 0 else ())
                        t_ekev = ac(lambda e, mc=mc: e.activation(
                            EKU[:, mc, 1:TTW], PA[mc][:, 1:TTW], AF.Exp), sem=ACT_S)
                    else:
                        t_ekev = ac(lambda e, mc=mc: e.activation(
                            EKU[:, mc, :], PA[mc][:, :], AF.Exp),
                            waits=[t_kpe, t_uall] if mc == 0 else (), sem=ACT_S)
                    t_ek_mc.append(t_ekev)
                t_mv = mixes(5, 1, t_kpe)
                for mc in range(FC):
                    for kc in range(FC):
                        t_vpe = te(lambda e, mc=mc, kc=kc: e.matmul(
                            PA[mc][:, :], WVb[:, kc, mc * 128:(mc + 1) * 128], MIXb[:, kc, :],
                            start=(kc == 0), stop=(kc == FC - 1)),
                            waits=(([t_mv, t_wv] if mc == 0 else []) + [t_ek_mc[mc]])
                            if kc == 0 else (),
                            sem=PE if kc == FC - 1 else None)
                t_ufc = []
                for fc in range(FC):
                    dv(lambda e, fc=fc: e.tensor_scalar(
                        EWT_A[:, :], ONESW[:, :], dcols[:, fc, 5:6], None, ALU.mult),
                       waits=[t_vpe, t_spill, t_spA, t_spB] if fc == 0 else ())
                    dv(lambda e, fc=fc: e.tensor_tensor_scan(
                        BSC[:, fc, :], EWT_A[:, :], EKU[:, fc, :], CARR[:, fc, 1:2],
                        ALU.mult, ALU.add))
                    dv(lambda e, fc=fc: e.tensor_copy(CARR[:, fc, 1:2], BSC[:, fc, TTW - 1:TTW]))
                    dv(lambda e, fc=fc: e.tensor_scalar(
                        BSC[:, fc, :], BSC[:, fc, :], WKV_EPS, None, ALU.add))
                    t_ufc.append(dv(lambda e, fc=fc: e.tensor_tensor(
                        EKU[:, fc, :], EKU[:, fc, :], PA[fc][:, :], ALU.mult), sem=DV))
                t_mr = mixes(6, 2, t_vpe)
                for mc in range(FC):
                    for kc in range(FC):
                        t_rpe = te(lambda e, mc=mc, kc=kc: e.matmul(
                            PA[mc][:, :], WRb[:, kc, mc * 128:(mc + 1) * 128], MIXb[:, kc, :],
                            start=(kc == 0), stop=(kc == FC - 1)),
                            waits=[t_mr, t_wr, t_ufc[mc]] if kc == 0 else (),
                            sem=PE if kc == FC - 1 else None)
                for fc in range(FC):
                    dv(lambda e, fc=fc: e.tensor_scalar(
                        EWT_A[:, :], ONESW[:, :], dcols[:, fc, 5:6], None, ALU.mult))
                    dv(lambda e, fc=fc: e.tensor_tensor_scan(
                        ASC[:, fc, :], EWT_A[:, :], EKU[:, fc, :], CARR[:, fc, 0:1],
                        ALU.mult, ALU.add))
                    t_uall = dv(lambda e, fc=fc: e.tensor_copy(
                        CARR[:, fc, 0:1], ASC[:, fc, TTW - 1:TTW]), sem=DV)
                for mc in range(FC):
                    t_rsev = ac(lambda e, mc=mc: e.activation(
                        RSS[:, mc, :], PA[mc][:, :], AF.Sigmoid),
                        waits=[t_rpe, t_spill] if mc == 0 else (), sem=ACT_S)
                tsl = slice(tt * TTW, (tt + 1) * TTW)
                t_spA = p.dma("sync", A_d[:, :, tsl].rearrange("fc q t -> q fc t"), ASC[:, :, :],
                      waits=[t_uall], sem=SP, stream="d20")
                t_spB = p.dma("sync", B_d[:, :, tsl].rearrange("fc q t -> q fc t"), BSC[:, :, :],
                      waits=[t_uall], sem=SP, stream="d21")
                t_spill = p.dma("sync", RS_d[:, :, tsl].rearrange("fc q t -> q fc t"),
                                RSS[:, :, :], waits=[t_rsev], sem=SP, stream="d22")

            for fc in range(FC):
                j, off = divmod(fc * 128, 512)
                t_r1bt = te(lambda e, fc=fc, j=j, off=off: e.transpose(
                    PA[j][0:1, off:off + 128], CARR[:, fc, 0:1], id32[:, :]),
                    waits=[t_rsev, t_uall] if fc == 0 else (), sem=None)
            for fc in range(FC):
                j, off = divmod(1024 + fc * 128, 512)
                t_r1bt = te(lambda e, fc=fc, j=j, off=off: e.transpose(
                    PA[j][0:1, off:off + 128], CARR[:, fc, 1:2], id32[:, :]),
                    sem=PE if fc == FC - 1 else None)
            for j in range(4):
                t_evb = dv(lambda e, j=j: e.tensor_copy(
                    SR1b[:, j * 512:(j + 1) * 512], PA[j][0:1, :]),
                    waits=[t_r1bt] if j == 0 else (), sem=DV)
            t_pub2 = p.dma("sync", r1b_in[:, :], SR1b[:, :], waits=[t_evb], sem=SP, stream="d23")
            t_r1b = p.op("gpsimd", lambda e: e.collective_compute(
                "AllGather", ALU.bypass, replica_groups=[list(range(NCORE))],
                ins=[r1b_in[:, :]], outs=[r1b_out[:, :]]), waits=[t_pub2], sem=CC, chain=False)
            for fc in range(FC):
                for q in range(2):
                    t_prv2 = p.op("gpsimd", lambda e, fc=fc, q=q: e.indirect_dma_start(
                        out=PRV2[:, fc, q:q + 1], out_offset=None,
                        in_=r1b_out.rearrange("c (d o) -> (c d) o", o=1),
                        in_offset=IndirectOffsetOnAxis(ap=pidx2[:, fc, q:q + 1], axis=0)),
                        waits=[t_r1b] if (fc == 0 and q == 0) else (),
                        sem=p.ssem("ind3"), amt=16, chain=False)
            t_acar = None
            for fc in range(FC):
                t_acar = dv(lambda e, fc=fc: e.tensor_scalar(
                    CARR[:, fc, 2:4], PRV2[:, fc, 0:2], f1c, None, ALU.mult),
                    waits=[t_prv2] if fc == 0 else (), sem=DV)

            gB = gate()
            pA.close()
            pB = ExitStack()
            sbb = lambda n, sh, d=F32: pB.enter_context(nc.sbuf_tensor(n, sh, d))
            LAs = [sbb(f"LA{j}", [128, FC, TTW]) for j in range(2)]
            LBs = [sbb("LBs", [128, FC, TTW])] * 2
            LRs = [sbb("LRs", [128, FC, TTW])] * 2
            XNTs = [sbb(f"XNT{j}", [128, FC, TTW]) for j in range(2)]
            RWBs = [sbb("RWBs", [128, FC, TTW], BF16)] * 2
            X2Ss = [sbb(f"X2S{j}", [128, FC, TTW]) for j in range(1)] * 2
            X2Ft = [sbb(f"X2F{j}", [128, FC, 128]) for j in range(2)]
            Z2T = sbb("Z2T", [128, D])
            X2TMb = [Z2T] * 2
            WOb = sbb("WOb", [128, FC, D], BF16)
            EWTT = Z2T[:, 0:TTW]
            EWT2 = [Z2T[:, TTW:2 * TTW], sbb("EWT2b", [128, TTW])]
            ST6b = sbb("ST6b", [128, 12])
            LN2H = sbb("LN2H", [128, NT, 4])
            LN2S = sbb("LN2S", [128, NT, 3])
            SR2 = X2TMb[0][0:1, :]
            PRV3 = sbb("PRV3", [128, FC])
            t_bn2w = [None, None]
            PB = [pB.enter_context(nc.psum_tensor(f"PB{j}", [128, 512], F32))
                  for j in range(8)]

            t_wo = p.dma("gpsimd", WOb[:, :, :], wo_h.rearrange("(kc q) o -> q kc o", q=128),
                         waits=gB, sem=LD, stream="wo")
            t_yev = None
            t_wope = None
            t_x2 = None
            t_x2sp = None
            t_rw_tt = []
            t_ld_h = [None, None]
            t_xn_h = [None, None]
            t_pl_h = [None, None]
            _x2info = {}
            t_x2 = None
            t_x2sp = None

            def emit_x2evac(tt2):
                nonlocal t_x2, t_x2sp
                wopetok, tsl2 = _x2info[tt2]
                for mc in range(FC):
                    t_x2 = dv(lambda e, mc=mc, tt2=tt2: e.tensor_tensor(
                        X2Ss[tt2 % 2][:, mc, :], PB[mc][:, :],
                        XNTs[tt2 % 2][:, mc, :], ALU.add),
                        waits=[wopetok, t_x2sp] if mc == 0 else (), sem=DV)
                t_x2sp = p.dma("sync", x2_d[:, :, tsl2].rearrange("fc q t -> q fc t"),
                               X2Ss[tt2 % 2][:, :, :], waits=[t_x2], sem=SP, stream="x2sp")
                t_xn_h[tt2 % 2] = t_x2

            t_lda = None
            for tt in range(NTT):
                tsl = slice(tt * TTW, (tt + 1) * TTW)
                lw = ([t_acar] + gB) if tt == 0 else ((t_ld_h[tt % 2] or []) + [t_lda])
                t_lda = p.dma("sync", LAs[tt % 2][:, :, :],
                              A_d[:, :, tsl].rearrange("fc q t -> q fc t"),
                              waits=lw, sem=SP, stream="ldA")
                t_ldb = p.dma("sync", LBs[tt % 2][:, :, :],
                              B_d[:, :, tsl].rearrange("fc q t -> q fc t"),
                              waits=[t_rw_tt[-1]] if t_rw_tt else [], sem=SP,
                              stream="ldB")
                t_ldr = p.dma("sync", LRs[tt % 2][:, :, :],
                              RS_d[:, :, tsl].rearrange("fc q t -> q fc t"),
                              waits=[t_rw_tt[-1]] if t_rw_tt else [], sem=SP,
                              stream="ldR")
                t_ldab = p.dma("sync", XNTs[tt % 2][:, :, :],
                               xnfm_d[:, :, tsl].rearrange("fc q t -> q fc t"),
                               waits=[t_xn_h[tt % 2]] if t_xn_h[tt % 2] is not None else [],
                               sem=SP, stream="ldXN")
                LA, LB, LR, XNT = LAs[tt % 2], LBs[tt % 2], LRs[tt % 2], XNTs[tt % 2]
                RWB = RWBs[tt % 2]
                t_rw = None
                for fc in range(FC):
                    dv(lambda e, fc=fc, tsl=tsl: e.tensor_scalar(
                        EWTT, ramp[:, tsl], dcols[:, fc, 4:5], None, ALU.mult),
                       waits=[t_ldab, t_lda, t_ldb, t_ldr, t_acar] if fc == 0 else ())
                    ac(lambda e: e.activation(EWTT, EWTT, AF.Exp),
                       waits=[(p.auto["vector"], p.auto["vector"].n)])
                    dv(lambda e, fc=fc, LA=LA: e.scalar_tensor_tensor(
                        LA[:, fc, :], EWTT, CARR[:, fc, 2:3], LA[:, fc, :],
                        ALU.mult, ALU.add), waits=[(p.auto["scalar"], p.auto["scalar"].n)])
                    dv(lambda e, fc=fc, LB=LB: e.scalar_tensor_tensor(
                        LB[:, fc, :], EWTT, CARR[:, fc, 3:4], LB[:, fc, :],
                        ALU.mult, ALU.add))
                    t_rec = dv(lambda e, fc=fc, LB=LB: e.reciprocal(
                        EWT2[fc % 2][:, :], LB[:, fc, :]),
                        waits=[t_pl_h[fc % 2]] if t_pl_h[fc % 2] is not None else (),
                        sem=DV)
                    gp(lambda e, fc=fc, LA=LA: e.tensor_tensor(
                        LA[:, fc, :], LA[:, fc, :], EWT2[fc % 2][:, :], ALU.mult),
                       waits=[t_rec])
                    t_rw = gp(lambda e, fc=fc, LA=LA, LR=LR, RWB=RWB: e.tensor_tensor(
                        RWB[:, fc, :], LR[:, fc, :], LA[:, fc, :], ALU.mult),
                        waits=[t_wope] if (fc == 0 and t_wope is not None) else (), sem=PL)
                    t_pl_h[fc % 2] = t_rw
                t_rw_tt.append(t_rw)
                if stage == "wkv":
                    t_wsp = p.dma("sync", RS_d[:, :, tsl].rearrange("fc q t -> q fc t"),
                                  LA[:, :, :], waits=[t_rw], sem=SP, stream="wkvsp")
                else:
                    t_wsp = t_rw
                if tt > 0:
                    emit_x2evac(tt - 1)
                for mc in range(FC):
                    for kc in range(FC):
                        t_wope = te(lambda e, mc=mc, kc=kc, RWB=RWB: e.matmul(
                            PB[mc][:, :], WOb[:, kc, mc * 128:(mc + 1) * 128], RWB[:, kc, :],
                            start=(kc == 0), stop=(kc == FC - 1)),
                            waits=[t_rw, t_wo, t_x2] if (mc == 0 and kc == 0) else (),
                            sem=PE if kc == FC - 1 else None)
                _x2info[tt] = (t_wope, tsl)
                t_ld_h[tt % 2] = [t_wsp]
            emit_x2evac(NTT - 1)
            t_ev2_h = [None, None]
            t_bn2 = None
            t_tr2 = None
            t_tr2_h = [None, None]
            for i in range(NT):
                pb0 = (i % 2) * 2
                t_ldx2b = p.dma("sync", X2Ft[i % 2][:, :, :],
                                x2_d[:, :, i * 128:(i + 1) * 128].rearrange("fc q t -> q fc t"),
                                waits=([t_x2sp] if i < 2 else [t_tr2_h[i % 2]]),
                                sem=SP, stream="x2ld")
                for fc in range(FC):
                    t_tr2 = te(lambda e, fc=fc, i=i, pb0=pb0: e.transpose(
                        PB[pb0 + fc // 4][:, (fc % 4) * 128:(fc % 4) * 128 + 128],
                        X2Ft[i % 2][:, fc, :], id32[:, :]),
                        waits=([t_ldx2b] + ([t_ev2_h[i % 2]] if t_ev2_h[i % 2] is not None
                                            else [])) if fc == 0 else (),
                        sem=PE if fc == FC - 1 else None)
                t_tr2_h[i % 2] = t_tr2
                ac(lambda e, i=i, pb0=pb0: e.copy(X2TMb[i % 2][:, 0:512], PB[pb0][:, :]),
                   waits=[t_tr2] + ([t_bn2] if i > 0 else []))
                t_ev2 = ac(lambda e, i=i, pb0=pb0: e.copy(X2TMb[i % 2][:, 512:1024],
                                                          PB[pb0 + 1][:, :]), sem=ACT_S)
                t_ev2_h[i % 2] = t_ev2
                dv(lambda e, i=i: e.bn_stats(ST6b[:, 0:6], X2TMb[i % 2][:, 0:512]),
                   waits=[t_ev2])
                dv(lambda e, i=i: e.bn_stats(ST6b[:, 6:12], X2TMb[i % 2][:, 512:1024]))
                dv(lambda e, i=i: e.bn_aggr(LN2H[:, i, 0:2], ST6b[:, 0:6]))
                t_bn2 = dv(lambda e, i=i: e.bn_aggr(LN2H[:, i, 2:4], ST6b[:, 6:12]), sem=DV)
                t_bn2w[i % 2] = t_bn2
            t_cmb = combine2(LN2H, LN2S[:, :, 0], LN2S[:, :, 1], t_bn2)
            t_rstd2 = rstd_batch(LN2S[:, :, 2], LN2S[:, :, 1], 1.0, LN_EPS, t_cmb)
            t_z2 = None
            t_yev = None
            t_yev_h = [None, None]
            t_zev_h = [None, None]
            t_zf_h = [None, None]
            for i in range(NT):
                pbf = 4 + (i % 2) * 2
                pbb = (i % 2) * 2
                t_ldx2c = p.dma("sync", X2Ft[i % 2][:, :, :],
                                x2_d[:, :, i * 128:(i + 1) * 128].rearrange("fc q t -> q fc t"),
                                waits=([t_tr2_h[i % 2]] if i < 2 else [t_zf_h[i % 2]]),
                                sem=SP, stream="x2ld2")
                for fc in range(FC):
                    t_trf2 = te(lambda e, fc=fc, i=i, pbf=pbf: e.transpose(
                        PB[pbf + fc // 4][:, (fc % 4) * 128:(fc % 4) * 128 + 128],
                        X2Ft[i % 2][:, fc, :], id32[:, :]),
                        waits=([t_ldx2c, t_rstd2, t_ev2, t_zev_h[i % 2]]
                               if fc == 0 else ()),
                        sem=PE if fc == FC - 1 else None)
                t_zf_h[i % 2] = t_trf2
                t_zev = ac(lambda e, i=i, pbf=pbf: e.copy(
                    Z2T[:, 0:512], PB[pbf][:, :]), waits=[t_trf2, t_z2])
                t_zev = ac(lambda e, i=i, pbf=pbf: e.copy(
                    Z2T[:, 512:1024], PB[pbf + 1][:, :]), sem=ACT_S)
                t_zev_h[i % 2] = t_zev
                t_z2 = dv(lambda e, i=i: e.tensor_scalar(
                    Z2T[:, :], Z2T[:, :],
                    LN2S[:, i, 0:1], LN2S[:, i, 2:3], ALU.subtract, ALU.mult),
                    waits=[t_zev], sem=DV)
                for fc in range(FC):
                    t_trb = te(lambda e, fc=fc, i=i, pbb=pbb: e.transpose(
                        PB[pbb + fc // 4][:, (fc % 4) * 128:(fc % 4) * 128 + 128],
                        Z2T[:, fc * 128:(fc + 1) * 128], id32[:, :]),
                        waits=([t_z2] + ([t_yev_h[i % 2]] if t_yev_h[i % 2] is not None
                                         else [])) if fc == 0 else (),
                        sem=PE if fc == FC - 1 else None)
                for fc in range(FC):
                    t_yev = ac(lambda e, fc=fc, i=i, pbb=pbb: e.activation(
                        XN1[:, fc, 8 + i * 128:8 + (i + 1) * 128],
                        PB[pbb + fc // 4][:, (fc % 4) * 128:(fc % 4) * 128 + 128],
                        AF.Identity, scale=vecs[:, fc, 2:3], bias=vecs[:, fc, 3:4]),
                        waits=[t_trb] if fc == 0 else (), sem=ACT_S)
                t_yev_h[i % 2] = t_yev
            t_yl = None
            for fc in range(FC):
                t_yl = dv(lambda e, fc=fc: e.tensor_copy(PRV3[:, fc:fc + 1],
                                                         XN1[:, fc, 7 + T:8 + T]),
                          waits=[t_yev] if fc == 0 else (), sem=DV)
            for fc in range(FC):
                j, off = divmod(fc * 128, 512)
                t_r2t = te(lambda e, fc=fc, j=j, off=off: e.transpose(
                    PB[j][0:1, off:off + 128], PRV3[:, fc:fc + 1], id32[:, :]),
                    waits=[t_yl] if fc == 0 else (), sem=PE if fc == FC - 1 else None)
            for j in range(2):
                t_evr2 = dv(lambda e, j=j: e.tensor_copy(SR2[:, j * 512:(j + 1) * 512],
                                                         PB[j][0:1, :]),
                            waits=[t_r2t] if j == 0 else (), sem=DV)
            t_pub3 = p.dma("sync", r2_in[:, :], SR2[:, :], waits=[t_evr2], sem=SP, stream="d31")
            t_r2 = p.op("gpsimd", lambda e: e.collective_compute(
                "AllGather", ALU.bypass, replica_groups=[list(range(NCORE))],
                ins=[r2_in[:, :]], outs=[r2_out[:, :]]), waits=[t_pub3], sem=CC, chain=False)
            for fc in range(FC):
                t_prv3 = p.op("gpsimd", lambda e, fc=fc: e.indirect_dma_start(
                    out=PRV3[:, fc:fc + 1], out_offset=None,
                    in_=r2_out.rearrange("c (d o) -> (c d) o", o=1),
                    in_offset=IndirectOffsetOnAxis(ap=pidx[:, fc:fc + 1], axis=0)),
                    waits=[t_r2] if fc == 0 else (), sem=p.ssem("ind4"), amt=16, chain=False)
            t_ynp = None
            for fc in range(FC):
                t_ynp = dv(lambda e, fc=fc: e.tensor_scalar(
                    XN1[:, fc, 7:8], PRV3[:, fc:fc + 1], f1c, None, ALU.mult),
                    waits=[t_prv3] if fc == 0 else (), sem=DV)

            if stage == "yn":
                tend = p.dma("gpsimd", out_h.rearrange("(q r) d -> q (r d)", q=128),
                             XN1[:, :, 8:8 + T], waits=[t_yev, t_ynp2] if False else [t_yev],
                             sem=ST, stream="yndump")
                p.wait("gpsimd", tend)
                pB.close()
                pmid.close()
                block = ctx.enter_context(nc.Block())
                p.realize(block)
                return nc
            g2 = gate()
            pB.close()
            p2 = ExitStack()
            sb2 = lambda n, sh, d=F32: p2.enter_context(nc.sbuf_tensor(n, sh, d))
            YMIX = sb2("YMIX", [128, FC, T], BF16)
            SFF = sb2("SFF", [128, FC, T])
            KRT = sb2("KRT", [128, 512])
            p2w = ExitStack()
            KKm = p2w.enter_context(nc.sbuf_tensor("KKm", [128, 4, T], BF16))
            WFKs = [p2w.enter_context(nc.sbuf_tensor(f"WFKs{j}", [128, FC, 512], BF16))
                    for j in range(2)]
            WFVs = [p2w.enter_context(nc.sbuf_tensor(f"WFVs{j}", [128, 4, D], BF16))
                    for j in range(2)]
            PC = [p2.enter_context(nc.psum_tensor(f"PC{j}", [128, 512], F32))
                  for j in range(8)]

            t_ym = None
            for fc in range(FC):
                dv(lambda e, fc=fc: e.tensor_tensor(
                    SFF[:, fc, :], XN1[:, fc, 8:8 + T], XN1[:, fc, 7:7 + T], ALU.add),
                   waits=(g2 + [t_ynp]) if fc == 0 else ())
                t_ym = dv(lambda e, fc=fc: e.tensor_scalar(
                    YMIX[:, fc, :], SFF[:, fc, :], 0.5, None, ALU.mult), sem=DV)

            t_kkev = None
            t_kk_mcc = [None, None, None, None]
            t_sfe = None
            t_sfe_h = [None, None, None, None]
            t_dn2 = None
            t_wk_hist = [None, None]
            for mb in range(8):
                wfk, wfv = WFKs[mb % 2], WFVs[mb % 2]
                t_fr = t_wk_hist[mb % 2]
                t_wfkld = p.dma("gpsimd", wfk[:, :, :],
                                wfk_h[:, mb * 512:(mb + 1) * 512].rearrange(
                                    "(kc q) o -> q kc o", q=128),
                                waits=[t_fr] if t_fr is not None else g2, sem=LD,
                                stream="wfk")
                t_wfvld = p.dma("gpsimd", wfv[:, :, :],
                                wfv_h[mb * 512:(mb + 1) * 512, :].rearrange(
                                    "(g q) o -> q g o", q=128),
                                waits=[t_fr] if t_fr is not None else g2,
                                sem=LD, stream="wfv")
                for nt in range(4):
                    for mcc in range(4):
                        for kc in range(FC):
                            t_kkpe = te(lambda e, mcc=mcc, kc=kc, nt=nt, wfk=wfk: e.matmul(
                                PC[mcc][:, :], wfk[:, kc, mcc * 128:(mcc + 1) * 128],
                                YMIX[:, kc, nt * 512:(nt + 1) * 512],
                                start=(kc == 0), stop=(kc == FC - 1)),
                                waits=([t_ym, t_wfkld, t_wfvld, t_kkev, t_sfe, t_dn2]
                                       if (nt == 0 and mcc == 0) else
                                       ([t_kk_mcc[mcc]] if t_kk_mcc[mcc] is not None else []))
                                if kc == 0 else (),
                                sem=PE if kc == FC - 1 else None)
                    for mcc in range(4):
                        ac(lambda e, mcc=mcc: e.activation(
                            KRT[:, :], PC[mcc][:, :], AF.Relu),
                           waits=([t_kkpe] if mcc == 0 else []) +
                                 [(p.auto["vector"], p.auto["vector"].n)])
                        t_kkev = dv(lambda e, mcc=mcc, nt=nt: e.tensor_tensor(
                            KKm[:, mcc, nt * 512:(nt + 1) * 512], KRT[:, :], KRT[:, :],
                            ALU.mult),
                            waits=[(p.auto["scalar"], p.auto["scalar"].n)], sem=DV)
                        t_kk_mcc[mcc] = t_kkev
                for nt in range(4):
                    for oc in range(FC):
                        for sub in range(4):
                            t_dn2 = te(lambda e, oc=oc, sub=sub, nt=nt, wfv=wfv: e.matmul(
                                PC[4 + oc % 4][:, :], wfv[:, sub, oc * 128:(oc + 1) * 128],
                                KKm[:, sub, nt * 512:(nt + 1) * 512],
                                start=(sub == 0), stop=(sub == 3)),
                                waits=([t_kkev, t_sfe] if (nt == 0 and oc == 0) else
                                       ([t_sfe_h[oc % 4]] if t_sfe_h[oc % 4] is not None else []))
                                if sub == 0 else (),
                                sem=PE if sub == 3 else None)
                        if mb == 0:
                            t_sfe = dv(lambda e, oc=oc, nt=nt: e.tensor_copy(
                                SFF[:, oc, nt * 512:(nt + 1) * 512], PC[4 + oc % 4][:, :]),
                                waits=[t_dn2], sem=DV)
                            t_sfe_h[oc % 4] = t_sfe
                        else:
                            t_sfe = dv(lambda e, oc=oc, nt=nt: e.tensor_tensor(
                                SFF[:, oc, nt * 512:(nt + 1) * 512],
                                SFF[:, oc, nt * 512:(nt + 1) * 512],
                                PC[4 + oc % 4][:, :], ALU.add),
                                waits=[t_dn2], sem=DV)
                        t_sfe_h[oc % 4] = t_sfe
                t_wk_hist[mb % 2] = t_dn2

            g2b = gate()
            p2w.close()
            WFRb = p2.enter_context(nc.sbuf_tensor("WFRb", [128, FC, D], BF16))
            RRTt = p2.enter_context(nc.sbuf_tensor("RRTt", [128, FC, 512], F32))
            X2Tb = p2.enter_context(nc.sbuf_tensor("X2Tb", [128, FC, 512], F32))
            SST = p2.enter_context(nc.sbuf_tensor("SST", [128, 512], F32))
            STMb = p2.enter_context(nc.sbuf_tensor("STMb", [128, D], F32))
            t_wfr = p.dma("gpsimd", WFRb[:, :, :],
                          wfr_h.rearrange("(kc q) o -> q kc o", q=128),
                          waits=g2b, sem=LD, stream="wfr")
            t_sev = None
            t_stm = None
            t_ssp = None
            for tt in range(NTT):
                tsl = slice(tt * 512, (tt + 1) * 512)
                cur = slice(8 + tt * 512, 8 + (tt + 1) * 512)
                t_ldx2 = p.dma("sync", X2Tb[:, :, :],
                               x2_d[:, :, tsl].rearrange("fc q t -> q fc t"),
                               waits=(g2b if tt == 0 else [t_stm]), sem=SP, stream="x2r")
                for mc in range(FC):
                    for kc in range(FC):
                        t_rrpe = te(lambda e, mc=mc, kc=kc, cur=cur: e.matmul(
                            PC[mc][:, :], WFRb[:, kc, mc * 128:(mc + 1) * 128],
                            XN1[:, kc, cur], start=(kc == 0), stop=(kc == FC - 1)),
                            waits=[t_wfr, t_sfe, t_kkev, t_sev, t_stm,
                                   (p.auto["scalar"], p.auto["scalar"].n)]
                            if (mc == 0 and kc == 0) else (),
                            sem=PE if kc == FC - 1 else None)
                t_rrev = None
                for mc in range(FC):
                    t_rrev = ac(lambda e, mc=mc: e.activation(
                        RRTt[:, mc, :], PC[mc][:, :], AF.Sigmoid),
                        waits=[t_rrpe] if mc == 0 else (), sem=ACT_S)
                for mc in range(FC):
                    dv(lambda e, mc=mc, tsl=tsl: e.tensor_tensor(
                        SST[:, :], RRTt[:, mc, :], SFF[:, mc, tsl], ALU.mult),
                       waits=[t_rrev, t_ldx2] if mc == 0 else ())
                    t_sev = dv(lambda e, mc=mc: e.tensor_tensor(
                        X2Tb[:, mc, :], SST[:, :], X2Tb[:, mc, :], ALU.add), sem=DV)
                for tg in range(4):
                    for fc in range(FC):
                        t_stp = te(lambda e, fc=fc, tg=tg: e.transpose(
                            PC[(tg % 2) * 2 + (fc // 4)][:, (fc % 4) * 128:(fc % 4) * 128 + 128],
                            X2Tb[:, fc, tg * 128:(tg + 1) * 128], id32[:, :]),
                            waits=[t_sev, t_stm] if fc == 0 else (),
                            sem=PE if fc == FC - 1 else None)
                    b0, b1 = (tg % 2) * 2, (tg % 2) * 2 + 1
                    ac(lambda e, b0=b0: e.copy(STMb[:, 0:512], PC[b0][:, :]),
                       waits=[t_stp, t_ssp])
                    t_stm = ac(lambda e, b1=b1: e.copy(STMb[:, 512:1024], PC[b1][:, :]),
                               sem=ACT_S)
                    t_ssp = p.dma("sync", stm_d[(tt * 4 + tg) * 128:(tt * 4 + tg + 1) * 128, :],
                                  STMb[:, :], waits=[t_stm], sem=SP, stream="sspill")
            if stage in ("s", "wkv", "x2"):
                src = {"s": stm_d, "wkv": RS_d, "x2": x2_d}[stage]
                if stage == "s":
                    tend = p.dma("sync", out_h.rearrange("(o a) b -> o (a b)", o=1),
                                 src.rearrange("(o a) b -> o (a b)", o=1),
                                 waits=gate(), sem=ST, stream="d37")
                else:
                    tend = p.dma("sync", out_h.rearrange("(o a) b -> o (a b)", o=1),
                                 src.rearrange("(o f) q t -> o (f q t)", o=1),
                                 waits=gate(), sem=ST, stream="d38")
                p.wait("sync", tend)
                p2.close()
                pmid.close()
                block = ctx.enter_context(nc.Block())
                p.realize(block)
                return nc
            g3 = gate()
            p2.close()
            pmid.close()

            p3 = ExitStack()
            sb3 = lambda n, sh, d=F32: p3.enter_context(nc.sbuf_tensor(n, sh, d))
            EOTs = [ctx.enter_context(nc.sbuf_tensor(f"EOT{j}", [128, D], F32))
                    for j in range(2)]
            XID = sb3("XID", [128, CAP // 128], U32)
            XGF = sb3("XGF", [128, FC, CAP], BF16)
            EACC = sb3("EACC", [128, FC, CAP])
            HEs = sb3("HEs", [128, 4, 512], BF16)
            GS1 = sb3("GS1", [128, 512])
            RGB = [sb3(f"RGB{j}", [128, FC, 512], BF16) for j in range(2)]
            RUB = [sb3(f"RUB{j}", [128, FC, 512], BF16) for j in range(2)]
            RDB = [sb3(f"RDB{j}", [128, 4, D], BF16) for j in range(2)]
            p3t = ExitStack()
            TBL8 = p3t.enter_context(nc.sbuf_tensor("TBL8", [8, E * CAP // 8], F32))
            MYROW = p3t.enter_context(nc.sbuf_tensor("MYROW", [1, CAP], F32))
            IDXT = p3t.enter_context(nc.sbuf_tensor("IDXT", [128, CAP // 128], F32))

            t_tbl = p.dma("sync", TBL8[:, :], tbl_out.rearrange("(e q) o -> e (q o)", e=8),
                          waits=g3 + [t_ar], sem=LD, stream="d39")
            p3a = ExitStack()
            PDS = p3a.enter_context(nc.psum_tensor("PDS", [128, 512], F32))
            PX0 = p3a.enter_context(nc.psum_tensor("PX0", [128, 1024], BF16))
            PX1 = p3a.enter_context(nc.psum_tensor("PX1", [128, 1024], BF16))
            t_myr = None
            t_myr2 = None
            for j in range(CAP // 512):
                t_myr = te(lambda e, j=j: e.matmul(
                    PDS[0:1, :], aux8[:, 1:2], TBL8[:, j * 512:(j + 1) * 512],
                    start=True, stop=True),
                    waits=(g3 + [t_tbl]) if j == 0 else [t_myr2], sem=PE)
                t_myr2 = dv(lambda e, j=j: e.tensor_copy(MYROW[:, j * 512:(j + 1) * 512],
                                                         PDS[0:1, :]),
                            waits=[t_myr], sem=DV)
            t_myd = p.dma("sync", myrow_d.rearrange("(o q) z -> o (q z)", o=1), MYROW[:, :],
                          waits=[t_myr2], sem=SP, stream="d40")
            t_myb = p.dma("sync", IDXT[:, :], myrow_d.rearrange("(q j) o -> q (j o)", q=128),
                          waits=[t_myd], sem=SP, stream="d41")
            t_xid = dv(lambda e: e.tensor_copy(XID[:, :], IDXT[:, :]), waits=[t_myb], sem=DV)
            p3t.close()

            with nc.sbuf_tensor("XGT", [128, 4, D], BF16) as XGT:
                t_xge = None
                t_xtr = None
                t_xg = None
                for batch in range(CAP // 512):
                    for jj in range(4):
                        j = batch * 4 + jj
                        t_xg = p.op("gpsimd", lambda e, j=j, jj=jj: e.indirect_dma_start(
                            out=XGT[:, jj, :], out_offset=None,
                            in_=ag1_out[:, :],
                            in_offset=IndirectOffsetOnAxis(ap=XID[:, j:j + 1], axis=0)),
                            waits=([t_xid, t_ag1] if j == 0 else []) +
                                  ([t_xge] if (jj == 0 and t_xge is not None) else []),
                            sem=p.ssem("ind5"), amt=16, chain=False)
                    for jj in range(4):
                        j = batch * 4 + jj
                        px = PX0 if jj % 2 == 0 else PX1
                        for fc in range(FC):
                            t_xtr = te(lambda e, jj=jj, fc=fc, px=px: e.transpose(
                                px[:, fc * 128:(fc + 1) * 128],
                                XGT[:, jj, fc * 128:(fc + 1) * 128], idbf[:, :]),
                                waits=([t_xg, t_myr] if jj == 0 else
                                       ([t_xge] if jj >= 2 else [])) if fc == 0 else (),
                                sem=PE if fc == FC - 1 else None)
                        t_xge = dv(lambda e, j=j, px=px: e.tensor_copy(
                            XGF[:, :, j * 128:(j + 1) * 128].rearrange("q fc t -> q fc t"),
                            px.ap().rearrange("q (fc t) -> q fc t", fc=FC)),
                            waits=[t_xtr], sem=DV)
            p3a.close()
            p3ps = ExitStack()
            PD = [p3ps.enter_context(nc.psum_tensor(f"PD{j}", [128, 512], F32))
                  for j in range(8)]

            NST = CAP // 512
            t_hev = None
            t_hev_h = [None, None]
            t_dse = None
            t_dse2 = None
            t_eev = None
            t_dpe = None
            t_rgl = t_rul = t_rdl = None
            t_dpe_hist = [None, None]
            for mcb in range(8):
                rgb, rub, rdb = RGB[mcb % 2], RUB[mcb % 2], RDB[mcb % 2]
                t_free = t_dpe_hist[mcb % 2]
                t_rgl = p.dma("gpsimd", rgb[:, :, :],
                              rg_h[:, mcb * 512:(mcb + 1) * 512].rearrange(
                                  "(kc q) o -> q kc o", q=128),
                              waits=([t_free] if t_free is not None else g3), sem=LD,
                              stream="rg")
                t_rul = p.dma("gpsimd", rub[:, :, :],
                              ru_h[:, mcb * 512:(mcb + 1) * 512].rearrange(
                                  "(kc q) o -> q kc o", q=128), sem=LD, stream="ru")
                t_rdl = p.dma("gpsimd", rdb[:, :, :],
                              rd_h[mcb * 512:(mcb + 1) * 512, :].rearrange(
                                  "(g q) o -> q g o", q=128), sem=LD, stream="rd")
                for n in range(NST):
                    ssl = slice(n * 512, (n + 1) * 512)
                    for mc in range(4):
                        j0, j1 = (mc % 2) * 2, (mc % 2) * 2 + 1
                        for kc in range(FC):
                            te(lambda e, mc=mc, kc=kc, j0=j0, rgb=rgb, ssl=ssl: e.matmul(
                                PD[j0][:, :], rgb[:, kc, mc * 128:(mc + 1) * 128],
                                XGF[:, kc, ssl], start=(kc == 0), stop=(kc == FC - 1)),
                                waits=[t_rgl, t_rul, t_rdl, t_xge, t_hev, t_dse, t_eev]
                                if (n == 0 and mc == 0 and kc == 0) else
                                (([t_hev_h[mc % 2]] if t_hev_h[mc % 2] is not None else [])
                                 if kc == 0 else ()))
                        for kc in range(FC):
                            t_upe = te(lambda e, mc=mc, kc=kc, j1=j1, rub=rub, ssl=ssl: e.matmul(
                                PD[j1][:, :], rub[:, kc, mc * 128:(mc + 1) * 128],
                                XGF[:, kc, ssl], start=(kc == 0), stop=(kc == FC - 1)),
                                sem=PE if kc == FC - 1 else None)
                        ac(lambda e, j0=j0: e.activation(GS1[:, :], PD[j0][:, :], AF.Silu),
                           waits=[t_upe, (p.auto["vector"], p.auto["vector"].n)])
                        t_hev = dv(lambda e, mc=mc, j1=j1: e.tensor_tensor(
                            HEs[:, mc, :], GS1[:, :], PD[j1][:, :], ALU.mult),
                            waits=[(p.auto["scalar"], p.auto["scalar"].n)], sem=DV)
                        t_hev_h[mc % 2] = t_hev
                    for oc in range(FC):
                        for sub in range(4):
                            t_dpe = te(lambda e, oc=oc, sub=sub, rdb=rdb: e.matmul(
                                PD[4 + oc % 4][:, :] if False else PD[oc % 4 + 4][:, :],
                                rdb[:, sub, oc * 128:(oc + 1) * 128],
                                HEs[:, sub, :], start=(sub == 0), stop=(sub == 3)),
                                waits=([t_hev, t_dse2] if oc == 0 else [t_dse2])
                                if sub == 0 else (),
                                sem=PE if sub == 3 else None)
                        if mcb == 0:
                            t_dse2 = dv(lambda e, oc=oc, n=n: e.tensor_copy(
                                EACC[:, oc, n * 512:(n + 1) * 512], PD[oc % 4 + 4][:, :]),
                                waits=[t_dpe], sem=DV)
                        else:
                            t_dse2 = dv(lambda e, oc=oc, n=n: e.tensor_tensor(
                                EACC[:, oc, n * 512:(n + 1) * 512],
                                EACC[:, oc, n * 512:(n + 1) * 512],
                                PD[oc % 4 + 4][:, :], ALU.add),
                                waits=[t_dpe], sem=DV)
                    t_dse = t_dse2
                t_dpe_hist[mcb % 2] = t_dpe
            t_etr = None
            t_eev = None
            t_esp = None
            t_eev_h = [None, None]
            t_esp_h = [None, None]
            for j in range(CAP // 128):
                pb0 = (j % 2) * 2
                for fc in range(FC):
                    t_etr = te(lambda e, fc=fc, j=j, pb0=pb0: e.transpose(
                        PD[pb0 + fc // 4][:, (fc % 4) * 128:(fc % 4) * 128 + 128],
                        EACC[:, fc, j * 128:(j + 1) * 128], id32[:, :]),
                        waits=([t_dse, (p.auto["scalar"], p.auto["scalar"].n)]
                               if j < 2 else [t_eev_h[j % 2]]) if fc == 0 else (),
                        sem=PE if fc == FC - 1 else None)
                ac(lambda e, pb0=pb0, j=j: e.copy(EOTs[j % 2][:, 0:512], PD[pb0][:, :]),
                   waits=[t_etr] + ([t_esp_h[j % 2]] if t_esp_h[j % 2] is not None else []))
                t_eev = ac(lambda e, pb0=pb0, j=j: e.copy(EOTs[j % 2][:, 512:1024],
                                                          PD[pb0 + 1][:, :]), sem=ACT_S)
                t_eev_h[j % 2] = t_eev
                t_esp = p.dma("sync", ag4_in.rearrange("(q j) o -> q j o", j=CAP // 128)[:, j:j + 1, :],
                              EOTs[j % 2].ap().rearrange("q (j o) -> q j o", j=1),
                              waits=[t_eev], sem=SP, stream="eout")
                t_esp_h[j % 2] = t_esp
            t_ag4 = p.op("gpsimd", lambda e: e.collective_compute(
                "AllGather", ALU.bypass, replica_groups=[list(range(NCORE))],
                ins=[ag4_in[:, :]], outs=[ag4_out[:, :]]), waits=[t_esp], sem=CC, chain=False)

            g4 = gate()
            p3ps.close()
            p3.close()
            p4 = ExitStack()
            SACC = p4.enter_context(nc.sbuf_tensor("SACC", [128, FC, T], F32))
            p4w = ExitStack()
            sb4 = lambda n, sh, d=F32: p4w.enter_context(nc.sbuf_tensor(n, sh, d))
            XBF = sb4("XBF", [128, FC, T], BF16)
            HB = sb4("HB", [128, 8, T], BF16)
            GS2 = sb4("GS2", [128, 512])
            SGB = [sb4(f"SGB{j}", [128, FC, 512], BF16) for j in range(2)]
            SUB = [sb4(f"SUB{j}", [128, FC, 512], BF16) for j in range(2)]
            SDB = [sb4(f"SDB{j}", [128, 4, D], BF16) for j in range(2)]
            PS = [p4.enter_context(nc.psum_tensor(f"PS{j}", [128, 512], F32))
                  for j in range(8)]
            t_xbf = p.dma("sync", XBF[:, :, :], xnbf_d[:, :, :].rearrange("fc q t -> q fc t"),
                          waits=[t_dpe_hist[0], t_dpe_hist[1], t_dse, t_etr],
                          sem=LD, stream="d46")
            NMB = DSH // 1024
            t_eevP3 = t_eev
            t_sde_h = [None, None]
            t_h2_nt = [None, None, None, None]
            t_h2 = None
            t_sde = None
            t_sgl = t_sul = t_sdl = None
            t_d2 = None
            for mb in range(NMB):
                for half in range(2):
                    mcb = mb * 2 + half
                    sgb, sub_ = SGB[mcb % 2], SUB[mcb % 2]
                    t_sgl = p.dma("gpsimd", sgb[:, :, :],
                                  sg_h[:, mcb * 512:(mcb + 1) * 512].rearrange(
                                      "(kc q) o -> q kc o", q=128),
                                  waits=[t_h2] if t_h2 is not None else
                                        [t_etr, t_eev_h[0], t_eev_h[1]],
                                  sem=LD, stream="d47")
                    t_sul = p.dma("gpsimd", sub_[:, :, :],
                                  su_h[:, mcb * 512:(mcb + 1) * 512].rearrange(
                                      "(kc q) o -> q kc o", q=128), sem=LD, stream="d48")
                    for mc in range(4):
                        for nt in range(4):
                            j0, j1 = nt, 4 + nt
                            for kc in range(FC):
                                gw = []
                                if kc == 0:
                                    if mc == 0 and nt == 0:
                                        gw += [t_sgl, t_sul, t_xbf, t_sde_h[0],
                                               t_sde_h[1], t_eevP3]
                                    if t_h2_nt[nt] is not None:
                                        gw.append(t_h2_nt[nt])
                                te(lambda e, mc=mc, kc=kc, nt=nt, j0=j0, sgb=sgb: e.matmul(
                                    PS[j0][:, :], sgb[:, kc, mc * 128:(mc + 1) * 128],
                                    XBF[:, kc, nt * 512:(nt + 1) * 512],
                                    start=(kc == 0), stop=(kc == FC - 1)),
                                    waits=gw)
                            for kc in range(FC):
                                t_u2 = te(lambda e, mc=mc, kc=kc, nt=nt, j1=j1, sub_=sub_: e.matmul(
                                    PS[j1][:, :], sub_[:, kc, mc * 128:(mc + 1) * 128],
                                    XBF[:, kc, nt * 512:(nt + 1) * 512],
                                    start=(kc == 0), stop=(kc == FC - 1)),
                                    waits=[t_sul] if (mc == 0 and nt == 0 and kc == 0) else (),
                                    sem=PE if kc == FC - 1 else None)
                            ac(lambda e, j0=j0: e.activation(GS2[:, :], PS[j0][:, :], AF.Silu),
                               waits=[t_u2, (p.auto["vector"], p.auto["vector"].n)])
                            t_h2 = dv(lambda e, half=half, mc=mc, nt=nt, j1=j1: e.tensor_tensor(
                                HB[:, half * 4 + mc, nt * 512:(nt + 1) * 512],
                                GS2[:, :], PS[j1][:, :], ALU.mult),
                                waits=[(p.auto["scalar"], p.auto["scalar"].n)], sem=DV)
                            t_h2_nt[nt] = t_h2
                if stage == "hb" and mb == int(os.environ.get('HBMB', '0')):
                    tend = p.dma("gpsimd", out_h.rearrange("(o a) b -> o (a b)", o=1).rearrange(
                        "o (q r) -> q (o r)", q=128), HB[:, :, :], waits=[t_h2], sem=ST, stream="d49")
                    p.wait("gpsimd", tend)
                    p4w.close()
                    p4.close()
                    block = ctx.enter_context(nc.Block())
                    p.realize(block)
                    return nc
                pass
                t_sdl0 = p.dma("gpsimd", SDB[0][:, :, :],
                               sd_h[mb * 1024:mb * 1024 + 512, :].rearrange(
                                   "(g q) o -> q g o", q=128),
                               waits=[t_d2] if t_d2 is not None else [], sem=LD, stream="d50")
                t_sdl = p.dma("gpsimd", SDB[1][:, :, :],
                              sd_h[mb * 1024 + 512:mb * 1024 + 1024, :].rearrange(
                                  "(g q) o -> q g o", q=128), sem=LD, stream="d51")
                t_sde_h = [t_sde, t_sde]
                for oc in range(8):
                    bs = (oc % 2) * 4
                    for kc2 in range(8):
                        for nt in range(4):
                            t_d2 = te(lambda e, kc2=kc2, oc=oc, bs=bs, nt=nt: e.matmul(
                                PS[bs + nt][:, :],
                                SDB[kc2 // 4][:, kc2 % 4, oc * 128:(oc + 1) * 128],
                                HB[:, kc2, nt * 512:(nt + 1) * 512],
                                start=(kc2 == 0), stop=(kc2 == 7)),
                                waits=([t_h2, t_sdl, t_sde_h[oc % 2]]
                                       if oc < 2 else [t_sde_h[oc % 2]])
                                if (kc2 == 0 and nt == 0) else (),
                                sem=PE if (kc2 == 7 and nt == 3) else None)
                    for nt in range(4):
                        if mb == 0:
                            t_sde = dv(lambda e, oc=oc, bs=bs, nt=nt: e.tensor_copy(
                                SACC[:, oc, nt * 512:(nt + 1) * 512], PS[bs + nt][:, :]),
                                waits=[t_d2] if nt == 0 else (), sem=DV)
                        else:
                            t_sde = dv(lambda e, oc=oc, bs=bs, nt=nt: e.tensor_tensor(
                                SACC[:, oc, nt * 512:(nt + 1) * 512],
                                SACC[:, oc, nt * 512:(nt + 1) * 512],
                                PS[bs + nt][:, :], ALU.add),
                                waits=[t_d2] if nt == 0 else (), sem=DV)
                    t_sde_h[oc % 2] = t_sde
                if stage == "shared" and os.environ.get("SHMB") is not None \
                        and mb == int(os.environ["SHMB"]):
                    t_shs = p.dma("sync", shd_d[:, :, :].rearrange("fc q t -> q fc t"),
                                  SACC[:, :, :], waits=[t_sde], sem=SP, stream="d52")
                    tend = p.dma("sync", out_h.rearrange("(o a) b -> o (a b)", o=1),
                                 shd_d.rearrange("(o f) q t -> o (f q t)", o=1),
                                 waits=[t_shs], sem=ST, stream="d53")
                    p.wait("sync", tend)
                    p4w.close()
                    p4.close()
                    block = ctx.enter_context(nc.Block())
                    p.realize(block)
                    return nc

            if stage == "xbf":
                t_shs = p.dma("gpsimd", shd_d[:, :, :].rearrange("fc q t -> q fc t"),
                              XBF[:, :, :], waits=[t_xbf], sem=SP, stream="d54")
                tend = p.dma("sync", out_h.rearrange("(o a) b -> o (a b)", o=1),
                             shd_d.rearrange("(o f) q t -> o (f q t)", o=1),
                             waits=[t_shs], sem=ST, stream="d55")
                p.wait("sync", tend)
                p4w.close()
                p4.close()
                block = ctx.enter_context(nc.Block())
                p.realize(block)
                return nc
            if stage == "shared":
                t_shs = p.dma("sync", shd_d[:, :, :].rearrange("fc q t -> q fc t"),
                              SACC[:, :, :], waits=[t_sde], sem=SP, stream="d56")
                tend = p.dma("sync", out_h.rearrange("(o a) b -> o (a b)", o=1),
                             shd_d.rearrange("(o f) q t -> o (f q t)", o=1),
                             waits=[t_shs], sem=ST, stream="d57")
                p.wait("sync", tend)
                p4w.close()
                p4.close()
                block = ctx.enter_context(nc.Block())
                p.realize(block)
                return nc

            g5 = gate()
            p4w.close()
            SHTM = p4.enter_context(nc.sbuf_tensor("SHTM", [128, D], F32))
            G1 = p4.enter_context(nc.sbuf_tensor("G1", [128, D], F32))
            G2 = p4.enter_context(nc.sbuf_tensor("G2", [128, D], F32))
            RTD = p4.enter_context(nc.sbuf_tensor("RTD", [128, D], F32))
            SRT = p4.enter_context(nc.sbuf_tensor("SRT", [128, D], F32))
            XRT = p4.enter_context(nc.sbuf_tensor("XRT", [128, D], F32))
            OUTT = p4.enter_context(nc.sbuf_tensor("OUTT", [128, D], F32))
            t_g12 = None
            t_oev = None
            t_osp = None
            t_ftr = None
            for i in range(NT):
                t_g12 = p.op("gpsimd", lambda e, i=i: e.indirect_dma_start(
                    out=G1[:, :], out_offset=None, in_=ag4_out[:, :],
                    in_offset=IndirectOffsetOnAxis(ap=IDX1[:, i:i + 1], axis=0)),
                    waits=(g5 + [t_ag4] if i == 0 else [t_g12]) + ([t_oev] if t_oev else []),
                    sem=p.ssem("ind6"), amt=16, chain=False)
                t_g12 = p.op("gpsimd", lambda e, i=i: e.indirect_dma_start(
                    out=G2[:, :], out_offset=None, in_=ag4_out[:, :],
                    in_offset=IndirectOffsetOnAxis(ap=IDX2[:, i:i + 1], axis=0)),
                    sem=p.ssem("ind7"), amt=16, chain=False)
                t_lds2 = p.dma("sync", SRT[:, :], stm_d[i * 128:(i + 1) * 128, :],
                               waits=[t_oev] if t_oev else g5, sem=SP, stream="d58")
                t_ldx3 = p.dma("sync", XRT[:, :], x_h[i * 128 * 0 + 0:0 + 128, :]
                               if False else
                               x_h.rearrange("(i q) f -> i q f", q=128)[i, :, :],
                               sem=SP, stream="d59")
                for fc in range(FC):
                    t_ftr = te(lambda e, fc=fc, i=i: e.transpose(
                        PS[fc // 4][:, (fc % 4) * 128:(fc % 4) * 128 + 128],
                        SACC[:, fc, i * 128:(i + 1) * 128], id32[:, :]),
                        waits=[t_sde, t_oev] if fc == 0 else (),
                        sem=PE if fc == FC - 1 else None)
                ac(lambda e: e.copy(SHTM[:, 0:512], PS[0][:, :]),
                   waits=[t_ftr] + ([t_oev] if t_oev else []))
                t_she = ac(lambda e: e.copy(SHTM[:, 512:1024], PS[1][:, :]), sem=ACT_S)
                dv(lambda e, i=i: e.tensor_scalar(RTD[:, :], G1[:, :], W1[:, i:i + 1],
                                                  None, ALU.mult),
                   waits=[t_g12, t_she])
                dv(lambda e, i=i: e.scalar_tensor_tensor(RTD[:, :], G2[:, :], W2[:, i:i + 1],
                                                         RTD[:, :], ALU.mult, ALU.add))
                t_ffn = dv(lambda e: e.tensor_tensor(RTD[:, :], RTD[:, :], SHTM[:, :], ALU.add),
                           sem=DV)
                t_gsl = ac(lambda e: e.activation(RTD[:, :], RTD[:, :], AF.Silu),
                           waits=[t_ffn], sem=ACT_S)
                dv(lambda e: e.tensor_tensor(OUTT[:, :], RTD[:, :], SRT[:, :], ALU.mult),
                   waits=[t_gsl, t_lds2, t_ldx3, t_osp])
                t_oev = dv(lambda e: e.tensor_tensor(OUTT[:, :], OUTT[:, :], XRT[:, :], ALU.add),
                           sem=DV)
                t_osp = p.dma("sync", out_h[i * 128:(i + 1) * 128, :], OUTT[:, :],
                              waits=[t_oev], sem=ST, stream="d60")
            p.wait("sync", t_osp)
            p4.close()
        else:
            raise NotImplementedError(stage)

        block = ctx.enter_context(nc.Block())
        p.realize(block)
    return nc


def _host_prep(inputs):
    x = np.asarray(inputs["L0_attn_x"], np.float32)
    vec_names = ["L0_attn_ln1_g", "L0_attn_ln1_b", "L0_attn_ln2_g", "L0_attn_ln2_b",
                 "L0_attn_mix_k", "L0_attn_mix_v", "L0_attn_mix_r",
                 "L0_attn_time_decay", "L0_attn_time_first"]
    vecs = np.stack([np.asarray(inputs[n], np.float32).reshape(FC, 128).T
                     for n in vec_names], axis=2)
    shared = dict(
        wk=np.asarray(inputs["L0_attn_Wk"], np.float32),
        wv=np.asarray(inputs["L0_attn_Wv"], np.float32),
        wr=np.asarray(inputs["L0_attn_Wr"], np.float32),
        wo=np.asarray(inputs["L0_attn_Wo"], np.float32),
        wfk=np.asarray(inputs["L0_attn_Wfk"], np.float32),
        wfv=np.asarray(inputs["L0_attn_Wfv"], np.float32),
        wfr=np.asarray(inputs["L0_attn_Wfr"], np.float32),
        sg=np.asarray(inputs["L0_ffn_sg"], np.float32),
        su=np.asarray(inputs["L0_ffn_su"], np.float32),
        sd=np.asarray(inputs["L0_ffn_sd"], np.float32),
        rwt=np.ascontiguousarray(np.asarray(inputs["L0_ffn_router_w"], np.float32).T),
        vecs=vecs,
        ident32=np.eye(128, dtype=np.float32),
        ident8=np.eye(8, dtype=np.float32),
        identbf=np.eye(128, dtype=ml_dtypes.bfloat16),
    )
    rg = np.asarray(inputs["L0_ffn_rg"], np.float32)
    ru = np.asarray(inputs["L0_ffn_ru"], np.float32)
    rd = np.asarray(inputs["L0_ffn_rd"], np.float32)
    pp = np.arange(128, dtype=np.uint32)
    per_core = []
    for c in range(NCORE):
        b, h = c >> 1, c & 1
        im = dict(shared)
        im["x"] = np.ascontiguousarray(x[b, h * T:(h + 1) * T, :])
        im["rg"] = np.ascontiguousarray(rg[c])
        im["ru"] = np.ascontiguousarray(ru[c])
        im["rd"] = np.ascontiguousarray(rd[c])
        aux = np.zeros((128, 8), np.float32)
        aux[:, 0] = 1.0 - h
        aux[:, 1] = float(h)
        aux[:, 2] = float(c * T)
        im["aux"] = aux
        a8 = np.zeros((8, 8), np.float32)
        a8[:, 0] = (np.arange(8) < c).astype(np.float32)
        a8[:, 1] = (np.arange(8) == c).astype(np.float32)
        a8[:, 2] = np.arange(8, dtype=np.float32) * CAP
        im["aux8"] = a8
        pi = np.zeros((128, FC), np.uint32)
        base = (c - 1) * 1024 if c > 0 else 0
        for fc in range(FC):
            pi[:, fc] = base + fc * 128 + pp
        im["previdx"] = pi
        pi2 = np.zeros((128, FC, 2), np.uint32)
        base2 = (c - 1) * 2048 if c > 0 else 0
        for fc in range(FC):
            pi2[:, fc, 0] = base2 + fc * 128 + pp
            pi2[:, fc, 1] = base2 + 1024 + fc * 128 + pp
        im["previdx2"] = pi2
        per_core.append(im)
    return per_core


def kernel(**inputs):
    per_core = _host_prep(inputs)
    nc = build(_STAGE)
    res = run_bass_kernel_spmd(nc, per_core, core_ids=list(range(NCORE)))
    outs = [np.asarray(r["out"]) for r in res.results]
    return np.stack(outs).reshape(4, 4096, 1024)
